# version 18
# baseline (speedup 1.0000x reference)
import dataclasses
import contextlib
import numpy as np
import ml_dtypes
import concourse.bass as bass
import concourse.mybir as mybir
from concourse.bass_utils import run_bass_kernel_spmd

F32 = mybir.dt.float32
BF16 = mybir.dt.bfloat16
AF = mybir.ActivationFunctionType
OP = mybir.AluOpType

B, T, C = 2, 8192, 2048
HPC = 4
D = 128
I = 8192
QT = 512
NT = T // QT
TPQ = 2048
TB = 512
NTB = TPQ // TB
NCH = C // 128
EXP_SHIFT = -12.0
EXP_SCALE = float(1.0 / np.sqrt(np.float64(D)))

_STAGE = 4
_DEBUG = False


class Prog:
    ENGINES = ("tensor", "vector", "scalar", "gpsimd", "sync")

    def __init__(self):
        self.ops = {e: [] for e in self.ENGINES}
        self.cnt = {}

    def count(self, sem):
        return self.cnt.get(sem.name, 0)

    def emit(self, eng, build, waits=(), incs=()):
        incs = tuple(incs)
        for s, n in incs:
            self.cnt[s.name] = self.cnt.get(s.name, 0) + n
        self.ops[eng].append((build, tuple(waits), incs))

    def wait(self, eng, s, v):
        self.ops[eng].append((None, ((s, v),), ()))

    def replay(self, name, eng):
        for build, waits, incs in self.ops[name]:
            for s, v in waits:
                if v > 0:
                    eng.wait_ge(s, v)
            if build is None:
                continue
            ins = build(eng)
            for s, n in incs:
                ins = ins.then_inc(s, n)


def build_program():
    nc = bass.Bass("TRN2", target_bir_lowering=False, debug=False)

    x_tiles = nc.dram_tensor("x_tiles", [NT * 128, NCH * QT], F32, kind="ExternalInput").ap()
    x_res = nc.dram_tensor("x_res", [C, TPQ], F32, kind="ExternalInput").ap()
    wq_h = nc.dram_tensor("wq_h", [128, NCH * HPC * D], F32, kind="ExternalInput").ap()
    wk_h = nc.dram_tensor("wk_h", [128, NCH * D], F32, kind="ExternalInput").ap()
    wv_h = nc.dram_tensor("wv_h", [128, NCH * D], F32, kind="ExternalInput").ap()
    wo_h = nc.dram_tensor("wo_h", [128, NCH * C], F32, kind="ExternalInput").ap()
    wg_h = nc.dram_tensor("wg_h", [64 * 128, NCH * 128], F32, kind="ExternalInput").ap()
    wu_h = nc.dram_tensor("wu_h", [64 * 128, NCH * 128], F32, kind="ExternalInput").ap()
    wd_h = nc.dram_tensor("wd_h", [I, C], F32, kind="ExternalInput").ap()
    cos_in = nc.dram_tensor("cos_t", [D, T], F32, kind="ExternalInput").ap()
    sin_in = nc.dram_tensor("sin_t", [D, T], F32, kind="ExternalInput").ap()
    masks_in = nc.dram_tensor("masks", [128, 4 * QT], BF16, kind="ExternalInput").ap()
    ones_in = nc.dram_tensor("ones_bf", [128, 1], BF16, kind="ExternalInput").ap()
    cvec_in = nc.dram_tensor("cvec", [128, 8], F32, kind="ExternalInput").ap()

    out_fm = nc.dram_tensor("out_fm", [C, TPQ], F32, kind="ExternalOutput").ap()

    contrib = [nc.dram_tensor(f"contrib{q}", [HPC * D, TPQ], BF16, kind="Internal").ap()
               for q in range(4)]
    gathered = nc.dram_tensor("gathered", [4 * C, TPQ], BF16, kind="Internal").ap()
    zscr = nc.dram_tensor("zscr", [NT * HPC, QT], F32, kind="Internal").ap()
    r1_dram = nc.dram_tensor("r1_dram", [C, TPQ], F32, kind="Internal").ap()
    x1_dram = nc.dram_tensor("x1_dram", [C, TPQ], F32, kind="Internal").ap()
    hsp_dram = nc.dram_tensor("hsp_dram", [NTB * I, TB], BF16, kind="Internal").ap()
    wd_bf = nc.dram_tensor("wd_bf", [2 * I, 1024], BF16, kind="Internal").ap()
    ln_dram = nc.dram_tensor("ln_dram", [2, 2 * TPQ], F32, kind="Internal").ap()

    dbg = {}
    if _DEBUG:
        if _STAGE == 1:
            dbg["q"] = nc.dram_tensor("dbg_q", [128, HPC * T], BF16, kind="ExternalOutput").ap()
            dbg["k"] = nc.dram_tensor("dbg_k", [128, T], BF16, kind="ExternalOutput").ap()
            dbg["v"] = nc.dram_tensor("dbg_v", [128, T], BF16, kind="ExternalOutput").ap()
        if _STAGE == 2:
            dbg["gat"] = nc.dram_tensor("dbg_gat", [4 * C, TPQ], BF16, kind="ExternalOutput").ap()
        if _STAGE == 3:
            dbg["x1"] = nc.dram_tensor("dbg_x1", [C, TPQ], F32, kind="ExternalOutput").ap()
        if _STAGE == 35:
            dbg["hsp"] = nc.dram_tensor("dbg_hsp", [I, TPQ], BF16, kind="ExternalOutput").ap()
        if _STAGE == 37:
            dbg["r2"] = nc.dram_tensor("dbg_r2", [C, TPQ], F32, kind="ExternalOutput").ap()

    P = Prog()
    stacks = []

    def push():
        s = contextlib.ExitStack()
        stacks.append(s)
        return s

    st = push()
    sem = {}
    for name in ("dMC", "dW1", "dXT", "dCS", "pQ", "pK", "pV", "vQ", "vK", "vV",
                 "xtF", "csF", "za", "zm",
                 "sp", "xp", "mk", "pv", "rc", "ev", "dZ", "dCT", "cc",
                 "dWO", "dGT", "dXR", "pWO", "vR1", "dR1w", "pR1",
                 "pST1", "pSQ1", "vLN1", "dLN1", "dR1r", "vX1", "dX1w", "pX1",
                 "dGU", "pGU", "aSI", "vH", "vAB", "vSP", "dHW",
                 "dHR", "dWD", "pWD", "pFB", "vR2", "dWC", "dWB", "dR2w", "pR2", "dX1r",
                 "pST2", "pSQ2", "vLN2", "dLN2", "dR2r", "dOUT", "dbgs",
                 "aSQ", "zS2", "vF", "dWD2", "sS2", "ccd"):
        sem[name] = st.enter_context(nc.semaphore(name))

    ones_sb = st.enter_context(nc.sbuf_tensor("ones_sb", [128, 1], BF16))
    cvec_sb = st.enter_context(nc.sbuf_tensor("cvec_sb", [128, 8], F32))

    P.emit("sync", lambda e: e.dma_start(out=ones_sb[:], in_=ones_in), incs=[(sem["dMC"], 16)])
    P.emit("sync", lambda e: e.dma_start(out=cvec_sb[:], in_=cvec_in), incs=[(sem["dMC"], 16)])
    consts_ready = P.count(sem["dMC"])

    c_exp = cvec_sb[:, 0:1]
    c_eps = cvec_sb[0:1, 1:2]
    c_i2048 = cvec_sb[0:1, 2:3]
    c_neg1 = cvec_sb[:, 3:4]
    c_ones32 = cvec_sb[:, 4:5]

    sA = push()
    q_all = sA.enter_context(nc.sbuf_tensor("q_all", [128, HPC * T], BF16, side="right"))
    sA2 = push()
    k_all = sA2.enter_context(nc.sbuf_tensor("k_all", [128, T], BF16, side="right"))
    v_all = sA2.enter_context(nc.sbuf_tensor("v_all", [128, T], BF16, side="right"))

    sB = push()
    wq_sb = sB.enter_context(nc.sbuf_tensor("wq_sb", [128, NCH * HPC * D], BF16))
    wk_sb = sB.enter_context(nc.sbuf_tensor("wk_sb", [128, NCH * D], BF16))
    wv_sb = sB.enter_context(nc.sbuf_tensor("wv_sb", [128, NCH * D], BF16))
    xt = [sB.enter_context(nc.sbuf_tensor(f"xt{i}", [128, NCH * QT], BF16)) for i in range(2)]
    cs = [sB.enter_context(nc.sbuf_tensor(f"cs{i}", [128, 2 * QT], F32)) for i in range(2)]
    rtmp1 = sB.enter_context(nc.sbuf_tensor("rtmp1", [128, QT], F32))
    rtmp2 = sB.enter_context(nc.sbuf_tensor("rtmp2", [128, QT], F32))

    pp1 = push()
    q_ps = [pp1.enter_context(nc.psum_tensor(f"q_ps{i}", [128, QT], F32)) for i in range(2)]
    k_ps = [pp1.enter_context(nc.psum_tensor(f"k_ps{i}", [128, QT], F32)) for i in range(2)]
    v_ps = [pp1.enter_context(nc.psum_tensor(f"v_ps{i}", [128, 128], F32)) for i in range(2)]

    P.emit("gpsimd", lambda e: e.dma_start(out=wq_sb[:], in_=wq_h), incs=[(sem["dW1"], 16)])
    P.emit("gpsimd", lambda e: e.dma_start(out=wk_sb[:], in_=wk_h), incs=[(sem["dW1"], 16)])
    P.emit("gpsimd", lambda e: e.dma_start(out=wv_sb[:], in_=wv_h), incs=[(sem["dW1"], 16)])
    w1_ready = P.count(sem["dW1"])

    def rope_emit(ps, dst, tau, psem, pval, vsem):
        cst = cs[tau % 2]
        cs_ready = 32 * (tau + 1)
        P.emit("vector", lambda e: e.tensor_mul(rtmp1[:], ps[:], cst[:, 0:QT]),
               waits=[(sem[psem], pval), (sem["dCS"], cs_ready)])
        P.emit("vector", lambda e: e.tensor_mul(rtmp2[0:64, :], ps[64:128, :], cst[0:64, QT:2 * QT]))
        P.emit("vector", lambda e: e.tensor_mul(rtmp2[64:128, :], ps[0:64, :], cst[64:128, QT:2 * QT]))
        P.emit("vector", lambda e: e.tensor_sub(dst[0:64, :], rtmp1[0:64, :], rtmp2[0:64, :]))
        P.emit("vector", lambda e: e.tensor_add(dst[64:128, :], rtmp1[64:128, :], rtmp2[64:128, :]),
               incs=[(sem[vsem], 1)])

    for tau in range(NT):
        xb = xt[tau % 2]
        t0 = QT * tau
        xw = [(sem["pV"], 4 * (tau - 1))] if tau >= 2 else []
        P.emit("gpsimd", lambda e, xb=xb, tau=tau: e.dma_start(
            out=xb[:], in_=x_tiles[tau * 128:(tau + 1) * 128, :]),
            waits=xw, incs=[(sem["dXT"], 16)])
        xt_ready = P.count(sem["dXT"])
        cw = [(sem["vK"], tau - 1)] if tau >= 2 else []
        csb = cs[tau % 2]
        P.emit("sync", lambda e, csb=csb, t0=t0: e.dma_start(out=csb[:, 0:QT], in_=cos_in[:, t0:t0 + QT]),
               waits=cw, incs=[(sem["dCS"], 16)])
        P.emit("sync", lambda e, csb=csb, t0=t0: e.dma_start(out=csb[:, QT:2 * QT], in_=sin_in[:, t0:t0 + QT]),
               incs=[(sem["dCS"], 16)])

        for h in range(HPC):
            ps = q_ps[h % 2]
            idx = tau * HPC + h
            pw = [(sem["dW1"], w1_ready), (sem["dXT"], xt_ready)]
            if idx >= 2:
                pw.append((sem["vQ"], idx - 1))
            for c in range(NCH):
                P.emit("tensor", lambda e, ps=ps, c=c, h=h, xb=xb: e.matmul(
                    ps[:], wq_sb[:, c * HPC * D + h * D: c * HPC * D + (h + 1) * D],
                    xb[:, c * QT:(c + 1) * QT],
                    start=(c == 0), stop=(c == NCH - 1), skip_group_check=True),
                    waits=pw if c == 0 else [],
                    incs=[(sem["pQ"], 1)] if c == NCH - 1 else [])
            rope_emit(ps, q_all[:, h * T + t0: h * T + t0 + QT], tau, "pQ", P.count(sem["pQ"]), "vQ")

        ps = k_ps[tau % 2]
        pw = [(sem["dW1"], w1_ready), (sem["dXT"], xt_ready)]
        if tau >= 2:
            pw.append((sem["vK"], tau - 1))
        for c in range(NCH):
            P.emit("tensor", lambda e, ps=ps, c=c, xb=xb: e.matmul(
                ps[:], wk_sb[:, c * D:(c + 1) * D], xb[:, c * QT:(c + 1) * QT],
                start=(c == 0), stop=(c == NCH - 1), skip_group_check=True),
                waits=pw if c == 0 else [],
                incs=[(sem["pK"], 1)] if c == NCH - 1 else [])
        rope_emit(ps, k_all[:, t0:t0 + QT], tau, "pK", P.count(sem["pK"]), "vK")

        for s in range(4):
            ps = v_ps[s % 2]
            idx = tau * 4 + s
            pw = [(sem["dW1"], w1_ready), (sem["dXT"], xt_ready)]
            if idx >= 2:
                pw.append((sem["vV"], idx - 1))
            for c in range(NCH):
                P.emit("tensor", lambda e, ps=ps, c=c, s=s, xb=xb: e.matmul(
                    ps[:], xb[:, c * QT + s * 128: c * QT + (s + 1) * 128],
                    wv_sb[:, c * D:(c + 1) * D],
                    start=(c == 0), stop=(c == NCH - 1), skip_group_check=True),
                    waits=pw if c == 0 else [],
                    incs=[(sem["pV"], 1)] if c == NCH - 1 else [])
            P.emit("vector", lambda e, ps=ps, t0=t0, s=s: e.tensor_copy(
                v_all[:, t0 + s * 128: t0 + (s + 1) * 128], ps[:]),
                waits=[(sem["pV"], P.count(sem["pV"]))],
                incs=[(sem["vV"], 1)])

    qkv_done = (P.count(sem["vQ"]), P.count(sem["vK"]), P.count(sem["vV"]))

    if _DEBUG and _STAGE == 1:
        P.emit("sync", lambda e: e.dma_start(out=dbg["q"], in_=q_all[:]),
               waits=[(sem["vQ"], qkv_done[0])], incs=[(sem["dbgs"], 16)])
        P.emit("sync", lambda e: e.dma_start(out=dbg["k"], in_=k_all[:]),
               waits=[(sem["vK"], qkv_done[1])], incs=[(sem["dbgs"], 16)])
        P.emit("sync", lambda e: e.dma_start(out=dbg["v"], in_=v_all[:]),
               waits=[(sem["vV"], qkv_done[2])], incs=[(sem["dbgs"], 16)])
    if _STAGE == 1:
        P.wait("sync", sem["dbgs"], P.count(sem["dbgs"]))
        P.wait("vector", sem["vV"], qkv_done[2])
        return finalize(nc, P, stacks)

    pp1.close()
    stacks.remove(pp1)
    pp2 = push()
    s_ps = [pp2.enter_context(nc.psum_tensor(f"s_ps{i}", [128, 2 * QT], F32)) for i in range(2)]
    o_ps = [pp2.enter_context(nc.psum_tensor(f"o_ps{i}", [128, QT], F32)) for i in range(2)]
    wo_ps = pp2.enter_context(nc.psum_tensor("wo_ps", [128, TB], F32))
    st_ps = pp2.enter_context(nc.psum_tensor("st_ps", [1, TB], F32))
    z_ps = [s_ps[0][0:1, 0:QT], s_ps[0][0:1, QT:2 * QT]]

    sB.close()
    stacks.remove(sB)
    sF = push()
    sq_sb = [sF.enter_context(nc.sbuf_tensor(f"sq{i}", [128, TB], BF16)) for i in range(2)]
    lnt_sb = sF.enter_context(nc.sbuf_tensor("lnt_sb", [1, 4 * TB], F32))
    rstd_b = sF.enter_context(nc.sbuf_tensor("rstd_b", [128, TPQ], F32))
    nu_b = sF.enter_context(nc.sbuf_tensor("nu_b", [128, TPQ], F32))
    x1t_sb = [sF.enter_context(nc.sbuf_tensor(f"x1t{i}", [128, TB], F32)) for i in range(2)]
    x1m_sb = [sF.enter_context(nc.sbuf_tensor(f"x1m{i}", [128, TB], F32)) for i in range(2)]
    r1t_sb = [sF.enter_context(nc.sbuf_tensor(f"r1t{i}", [128, TB], F32)) for i in range(2)]

    sC = push()
    masks_sb = sC.enter_context(nc.sbuf_tensor("masks_sb", [128, 4 * QT], BF16))
    p_sb = [sC.enter_context(nc.sbuf_tensor(f"p_sb{i}", [128, 2 * QT], BF16)) for i in range(2)]
    zacc_sb = [sC.enter_context(nc.sbuf_tensor(f"zacc{i}", [128, 2 * QT], F32)) for i in range(2)]
    zinv_sb = [sC.enter_context(nc.sbuf_tensor(f"zinv{i}", [1, QT], F32)) for i in range(2)]
    zb_sb = [sC.enter_context(nc.sbuf_tensor(f"zb{i}", [128, QT], F32)) for i in range(2)]
    oo_sb = [sC.enter_context(nc.sbuf_tensor(f"oo{i}", [128, QT], BF16)) for i in range(2)]
    ccd_sb = sC.enter_context(nc.sbuf_tensor("ccd_sb", [128, 1], BF16))
    gat_sb = sC.enter_context(nc.sbuf_tensor("gat_sb", [128, NCH * TB], BF16))
    sqq_sb = sC.enter_context(nc.sbuf_tensor("sqq_sb", [128, NCH * TB], BF16))
    xr_sb = [sC.enter_context(nc.sbuf_tensor(f"xr{i}", [128, TB], F32)) for i in range(2)]
    tbq_sb = [sC.enter_context(nc.sbuf_tensor(f"tbq{i}", [128, TB], BF16)) for i in range(2)]

    P.emit("sync", lambda e: e.dma_start(out=masks_sb[:], in_=masks_in),
           waits=[(sem["vV"], qkv_done[2])], incs=[(sem["dMC"], 16)])
    consts_ready = P.count(sem["dMC"])

    sW = push()
    wo_sb = [sW.enter_context(nc.sbuf_tensor(f"wo_sb{i}", [128, NCH * 128], BF16))
             for i in range(2)]

    P.wait("tensor", sem["vQ"], qkv_done[0])
    P.wait("tensor", sem["vK"], qkv_done[1])
    P.wait("tensor", sem["vV"], qkv_done[2])
    P.wait("tensor", sem["dMC"], 32)
    P.wait("scalar", sem["dMC"], consts_ready)
    P.wait("vector", sem["dMC"], consts_ready)

    pid = nc.partition_id([mybir.EngineType.SP])
    grp_off = (pid % 4) * TB

    def x1q(c, hb):
        o = (c // 4) * T + hb * TPQ + (c % 4) * TB
        return q_all[:, o:o + TB]

    def emit_post(qq):
        gw = [(sem["cc"], min(qq + 2, 4)), (sem["ccd"], min(qq + 2, 4))]
        if qq >= 1:
            gw.append((sem["pWO"], 256 * qq))
        src0 = gathered[0:128, 0:TB]
        src3 = dataclasses.replace(
            src0, offset=src0.offset + qq * (C * TPQ) + grp_off,
            ap=mybir.VecI64Pair([[TPQ, 128], [128 * TPQ, NCH], [1, TB]]))
        gb3 = gat_sb[:].rearrange("p (c n) -> p c n", c=NCH)
        P.emit("sync", lambda e, gb3=gb3, src3=src3: e.dma_start(out=gb3, in_=src3),
               waits=gw, incs=[(sem["dGT"], 16)])
        gt_ready = P.count(sem["dGT"])
        for c2 in range(NCH):
            ngrp = qq * NCH + c2
            wslot = wo_sb[ngrp % 2]
            loww = [(sem["pWO"], 16 * (ngrp - 1))] if ngrp >= 2 else []
            P.emit("gpsimd", lambda e, wslot=wslot, c2=c2: e.dma_start(
                out=wslot[:], in_=wo_h[:, c2 * C:(c2 + 1) * C]),
                waits=loww, incs=[(sem["dWO"], 16)])
            ps = wo_ps
            mw = [(sem["dWO"], P.count(sem["dWO"])), (sem["dGT"], gt_ready)]
            if ngrp >= 1:
                mw.append((sem["vR1"], ngrp))
            for c1 in range(NCH):
                P.emit("tensor", lambda e, c1=c1, wslot=wslot, ps=ps: e.matmul(
                    ps[:], wslot[:, c1 * 128:(c1 + 1) * 128],
                    gat_sb[:, c1 * TB:(c1 + 1) * TB],
                    start=(c1 == 0), stop=(c1 == NCH - 1), skip_group_check=True),
                    waits=mw if c1 == 0 else [],
                    incs=[(sem["pWO"], 1)])
            xrb = xr_sb[ngrp % 2]
            xw = [(sem["vR1"], ngrp - 1)] if ngrp >= 2 else []
            P.emit("gpsimd", lambda e, xrb=xrb, c2=c2, qq=qq: e.dma_start(
                out=xrb[:], in_=x_res[c2 * 128:(c2 + 1) * 128, qq * TB:(qq + 1) * TB]),
                waits=xw, incs=[(sem["dXR"], 16)])
            rt = r1t_sb[ngrp % 2]
            ww = [(sem["pWO"], 16 * (ngrp + 1)), (sem["dXR"], P.count(sem["dXR"]))]
            if ngrp >= 2:
                ww.append((sem["dR1w"], 16 * (ngrp - 1)))
                ww.append((sem["pR1"], ngrp - 1))
                ww.append((sem["vX1"], 16 * qq))
            P.emit("vector", lambda e, rt=rt, xrb=xrb, ps=ps: e.tensor_add(rt[:], ps[:], xrb[:]),
                   waits=ww, incs=[(sem["vR1"], 1)])
            vr = P.count(sem["vR1"])
            P.emit("sync", lambda e, rt=rt, c2=c2, qq=qq: e.dma_start(
                out=r1_dram[c2 * 128:(c2 + 1) * 128, qq * TB:(qq + 1) * TB], in_=rt[:]),
                waits=[(sem["vR1"], vr)], incs=[(sem["dR1w"], 16)])
            tb16 = tbq_sb[ngrp % 2]
            cw = [(sem["pST1"], ngrp - 1)] if ngrp >= 2 else []
            P.emit("vector", lambda e, tb16=tb16, rt=rt: e.tensor_copy(tb16[:], rt[:]),
                   waits=cw, incs=[(sem["pR1"], 1)])
            P.emit("vector", lambda e, tb16=tb16, c2=c2: e.tensor_mul(
                sqq_sb[:, c2 * TB:(c2 + 1) * TB], tb16[:], tb16[:]),
                incs=[(sem["pSQ1"], 1)])
            sw1 = [(sem["pR1"], P.count(sem["pR1"]))]
            if c2 == 0 and qq >= 1:
                sw1.append((sem["vLN1"], 5 * qq))
            P.emit("tensor", lambda e, tb16=tb16, c2=c2: e.matmul(
                st_ps[0:1, 0:TB], ones_sb[:, 0:1], tb16[:],
                start=(c2 == 0), stop=(c2 == NCH - 1), skip_group_check=True),
                waits=sw1, incs=[(sem["pST1"], 1)])
        l_mu = lnt_sb[0:1, 0:TB]
        l_m2 = lnt_sb[0:1, TB:2 * TB]
        l_t = lnt_sb[0:1, 2 * TB:3 * TB]
        l_r = lnt_sb[0:1, 3 * TB:4 * TB]
        stw = [(sem["pST1"], 16 * (qq + 1))]
        if qq >= 1:
            stw.append((sem["dLN1"], 64 * (qq - 1) + 32))
        P.emit("vector", lambda e: e.tensor_scalar(l_mu, st_ps[0:1, 0:TB], c_i2048, None, OP.mult),
               waits=stw, incs=[(sem["vLN1"], 1)])
        for c2 in range(NCH):
            s2w = ([(sem["vLN1"], 5 * qq + 1), (sem["pSQ1"], 16 * (qq + 1))]
                   if c2 == 0 else [])
            P.emit("tensor", lambda e, c2=c2: e.matmul(
                st_ps[0:1, 0:TB], ones_sb[:, 0:1], sqq_sb[:, c2 * TB:(c2 + 1) * TB],
                start=(c2 == 0), stop=(c2 == NCH - 1), skip_group_check=True),
                waits=s2w, incs=[(sem["sS2"], 1)])
        P.emit("vector", lambda e: e.tensor_scalar(l_m2, st_ps[0:1, 0:TB], c_i2048, None, OP.mult),
               waits=[(sem["sS2"], 16 * (qq + 1))], incs=[(sem["vLN1"], 1)])
        P.emit("vector", lambda e: e.tensor_mul(l_t, l_mu, l_mu))
        P.emit("vector", lambda e: e.tensor_sub(l_r, l_m2, l_t), incs=[(sem["vLN1"], 1)])
        P.emit("scalar", lambda e: e.activation(l_t, l_r, AF.Sqrt, bias=c_eps),
               waits=[(sem["vLN1"], 5 * qq + 3)], incs=[(sem["vLN1"], 1)])
        P.emit("vector", lambda e: e.reciprocal(l_r, l_t), waits=[(sem["vLN1"], 5 * qq + 4)])
        P.emit("vector", lambda e: e.tensor_mul(l_m2, l_mu, l_r), incs=[(sem["vLN1"], 1)])
        P.emit("sync", lambda e, qq=qq: e.dma_start(
            out=ln_dram[0:1, qq * TB:(qq + 1) * TB], in_=l_r),
            waits=[(sem["vLN1"], 5 * qq + 5)], incs=[(sem["dLN1"], 16)])
        P.emit("sync", lambda e, qq=qq: e.dma_start(
            out=ln_dram[1:2, qq * TB:(qq + 1) * TB], in_=l_m2),
            incs=[(sem["dLN1"], 16)])
        dlc = P.count(sem["dLN1"])
        P.emit("sync", lambda e, qq=qq: e.dma_start(
            out=rstd_b[:, qq * TB:(qq + 1) * TB],
            in_=ln_dram[0:1, qq * TB:(qq + 1) * TB].partition_broadcast(128)),
            waits=[(sem["dLN1"], dlc)], incs=[(sem["dLN1"], 16)])
        P.emit("sync", lambda e, qq=qq: e.dma_start(
            out=nu_b[:, qq * TB:(qq + 1) * TB],
            in_=ln_dram[1:2, qq * TB:(qq + 1) * TB].partition_broadcast(128)),
            incs=[(sem["dLN1"], 16)])
        dl1 = P.count(sem["dLN1"])
        for c2 in range(NCH):
            k = qq * NCH + c2
            rt = r1t_sb[k % 2]
            rw = [(sem["dLN1"], dl1), (sem["dR1w"], 256 * (qq + 1)),
                  (sem["pR1"], 16 * (qq + 1))]
            if k >= 2:
                rw.append((sem["vX1"], k - 1))
            P.emit("sync", lambda e, rt=rt, c2=c2, qq=qq: e.dma_start(
                out=rt[:], in_=r1_dram[c2 * 128:(c2 + 1) * 128, qq * TB:(qq + 1) * TB]),
                waits=rw, incs=[(sem["dR1r"], 16)])
            drc = P.count(sem["dR1r"])
            xm = x1m_sb[k % 2]
            xv = x1t_sb[k % 2]
            P.emit("vector", lambda e, xm=xm, rt=rt, qq=qq: e.tensor_mul(
                xm[:], rt[:], rstd_b[:, qq * TB:(qq + 1) * TB]),
                waits=[(sem["dR1r"], drc)])
            xw2 = []
            if k >= 2:
                xw2 = [(sem["dX1w"], 16 * (k - 1)), (sem["pX1"], k - 1)]
            P.emit("vector", lambda e, xv=xv, xm=xm, qq=qq: e.tensor_sub(
                xv[:], xm[:], nu_b[:, qq * TB:(qq + 1) * TB]),
                waits=xw2, incs=[(sem["vX1"], 1)])
            vxc = P.count(sem["vX1"])
            P.emit("sync", lambda e, xv=xv, c2=c2, qq=qq: e.dma_start(
                out=x1_dram[c2 * 128:(c2 + 1) * 128, qq * TB:(qq + 1) * TB], in_=xv[:]),
                waits=[(sem["vX1"], vxc)], incs=[(sem["dX1w"], 16)])
            P.emit("gpsimd", lambda e, xv=xv, c2=c2, qq=qq: e.tensor_copy(
                x1q(c2, qq), xv[:]),
                waits=[(sem["vX1"], vxc)], incs=[(sem["pX1"], 1)])

    n_pair = 0
    n_tile = 0
    n_evac = 0

    for t in range(NT):
        nkt = 4 * (t + 1)
        q0 = QT * t
        for p in range(2):
            base = n_tile

            def emit_qk(j, tile_idx, first2=False, q0=q0, p=p):
                sps = s_ps[tile_idx % 2]
                waits = []
                if tile_idx >= 2:
                    waits.append((sem["xp"], tile_idx - 1))
                if first2:
                    waits.append((sem["rc"], P.count(sem["rc"])))
                for hh in (0, 1):
                    h = 2 * p + hh
                    P.emit("tensor", lambda e, sps=sps, j=j, hh=hh, h=h, q0=q0: e.matmul(
                        sps[:, hh * QT:(hh + 1) * QT],
                        k_all[:, j * 128:(j + 1) * 128],
                        q_all[:, h * T + q0: h * T + q0 + QT],
                        start=True, stop=True, skip_group_check=True),
                        waits=waits if hh == 0 else [],
                        incs=[(sem["sp"], 1)] if hh == 1 else [])

            def emit_exp(tile_idx):
                sps = s_ps[tile_idx % 2]
                pb = p_sb[tile_idx % 2]
                waits = [(sem["sp"], tile_idx + 1)]
                if tile_idx >= 2:
                    waits.append((sem["pv"], tile_idx - 1))
                    waits.append((sem["za"], tile_idx - 1))
                P.emit("scalar", lambda e, sps=sps, pb=pb: e.activation(
                    pb[:], sps[:], AF.Exp, bias=c_exp, scale=EXP_SCALE),
                    waits=waits, incs=[(sem["xp"], 1)])

            def emit_mask(j, tile_idx, t=t):
                o = j - 4 * t
                pb = p_sb[tile_idx % 2]
                m = masks_sb[:, o * QT:(o + 1) * QT]
                mrep = dataclasses.replace(
                    m, ap=mybir.VecI64Pair([list(m.ap[0]), [0, 2], list(m.ap[1])]))
                pb3 = pb[:].rearrange("a (r n) -> a r n", r=2)
                P.emit("vector", lambda e, pb3=pb3, mrep=mrep: e.tensor_mul(pb3, pb3, mrep),
                       waits=[(sem["xp"], tile_idx + 1)], incs=[(sem["mk"], 1)])

            def emit_pvz(j, tile_idx, is_first, is_last, masked, n_pair=n_pair):
                pb = p_sb[tile_idx % 2]
                waits = ([(sem["mk"], P.count(sem["mk"]))] if masked
                         else [(sem["xp"], tile_idx + 1)])
                if is_first:
                    waits = waits + [(sem["ev"], 2 * n_pair)]
                for hh in (0, 1):
                    P.emit("tensor", lambda e, pb=pb, j=j, hh=hh, a=is_first, z=is_last: e.matmul(
                        o_ps[hh][:], v_all[:, j * 128:(j + 1) * 128], pb[:, hh * QT:(hh + 1) * QT],
                        start=a, stop=z, skip_group_check=True),
                        waits=waits if hh == 0 else [],
                        incs=[(sem["pv"], 1)] if hh == 1 else [])

            def emit_zadd(j, tile_idx, is_first, masked, n_pair=n_pair):
                pb = p_sb[tile_idx % 2]
                zb = zacc_sb[n_pair % 2]
                waits = ([(sem["mk"], P.count(sem["mk"]))] if masked
                         else [(sem["xp"], tile_idx + 1)])
                if is_first and n_pair >= 2:
                    waits = waits + [(sem["zm"], 2 * (n_pair - 1))]
                if is_first:
                    P.emit("vector", lambda e, zb=zb, pb=pb: e.tensor_copy(zb[:], pb[:]),
                           waits=waits, incs=[(sem["za"], 1)])
                else:
                    P.emit("vector", lambda e, zb=zb, pb=pb: e.tensor_add(zb[:], zb[:], pb[:]),
                           waits=waits, incs=[(sem["za"], 1)])

            emit_qk(0, base, first2=True)
            if nkt > 1:
                emit_qk(1, base + 1, first2=True)
            for j in range(nkt):
                ti = base + j
                masked = j >= nkt - 4
                emit_exp(ti)
                if masked:
                    emit_mask(j, ti)
                emit_pvz(j, ti, j == 0, j == nkt - 1, masked)
                emit_zadd(j, ti, j == 0, masked)
                if j + 2 < nkt:
                    emit_qk(j + 2, ti + 2)
            n_tile += nkt

            zac = P.count(sem["za"])
            zmb = zacc_sb[n_pair % 2]
            for hh in (0, 1):
                zw0 = [(sem["za"], zac), (sem["rc"], 2 * n_pair)] if hh == 0 else []
                P.emit("tensor", lambda e, hh=hh, zmb=zmb: e.matmul(
                    z_ps[hh], c_ones32, zmb[:, hh * QT:(hh + 1) * QT],
                    start=True, stop=True, skip_group_check=True),
                    waits=zw0, incs=[(sem["zm"], 1)])
            zmc = P.count(sem["zm"])
            for hh in (0, 1):
                h = 2 * p + hh
                zw = [(sem["zm"], zmc)]
                if n_evac >= 2:
                    zw.append((sem["dZ"], 32 * (n_evac - 2) + 16))
                P.emit("vector", lambda e, hh=hh, b=n_evac % 2: e.reciprocal(zinv_sb[b][:], z_ps[hh]),
                       waits=zw, incs=[(sem["rc"], 1)])
                rcc = P.count(sem["rc"])
                row = t * HPC + h
                P.emit("sync", lambda e, b=n_evac % 2, row=row: e.dma_start(
                    out=zscr[row:row + 1, :], in_=zinv_sb[b][:]),
                    waits=[(sem["rc"], rcc)], incs=[(sem["dZ"], 16)])
                dz1 = P.count(sem["dZ"])
                bw = [(sem["dZ"], dz1)]
                if n_evac >= 2:
                    bw.append((sem["ev"], n_evac - 1))
                P.emit("sync", lambda e, b=n_evac % 2, row=row: e.dma_start(
                    out=zb_sb[b][:], in_=zscr[row:row + 1, :].partition_broadcast(128)),
                    waits=bw, incs=[(sem["dZ"], 16)])
                dz2 = P.count(sem["dZ"])
                ww = [(sem["dZ"], dz2)]
                if n_evac >= 2:
                    ww.append((sem["dCT"], 16 * (n_evac - 1)))
                P.emit("vector", lambda e, hh=hh, b=n_evac % 2: e.tensor_mul(
                    oo_sb[b][:], o_ps[hh][:], zb_sb[b][:]),
                    waits=ww, incs=[(sem["ev"], 1)])
                evc = P.count(sem["ev"])
                qq, col = t // 4, (t % 4) * QT
                P.emit("sync", lambda e, h=h, qq=qq, col=col, b=n_evac % 2: e.dma_start(
                    out=contrib[qq][h * 128:(h + 1) * 128, col:col + QT], in_=oo_sb[b][:]),
                    waits=[(sem["ev"], evc)], incs=[(sem["dCT"], 16)])
                n_evac += 1
            n_pair += 1

        if t == 12:
            emit_post(0)
        elif t == 13:
            emit_post(1)
        if t % 4 == 3:
            qq = t // 4
            ctc = P.count(sem["dCT"])
            P.emit("gpsimd", lambda e, qq=qq: e.collective_compute(
                "AllGather", OP.bypass, [[0, 1, 2, 3], [4, 5, 6, 7]],
                [contrib[qq].opt()], [gathered[qq * C:(qq + 1) * C, :].opt()]),
                waits=[(sem["dCT"], ctc)], incs=[(sem["cc"], 1)])
            P.emit("gpsimd", lambda e: e.tensor_copy(ccd_sb[:], ones_sb[:]),
                   incs=[(sem["ccd"], 1)])

    emit_post(2)
    emit_post(3)

    if _DEBUG and _STAGE == 2:
        P.emit("sync", lambda e: e.dma_start(out=dbg["gat"], in_=gathered),
               waits=[(sem["cc"], 4)], incs=[(sem["dbgs"], 16)])
    if _STAGE == 2:
        P.wait("sync", sem["dbgs"], P.count(sem["dbgs"]))
        P.wait("gpsimd", sem["cc"], 4)
        return finalize(nc, P, stacks)

    x1_ready = P.count(sem["pX1"])
    nx1w = P.count(sem["dX1w"])

    pp2.close()
    stacks.remove(pp2)
    ppffa = push()
    g_ps = [ppffa.enter_context(nc.psum_tensor(f"g_ps{m}", [128, TB], F32)) for m in range(4)]
    u_ps = [ppffa.enter_context(nc.psum_tensor(f"u_ps{m}", [128, TB], F32)) for m in range(4)]

    sA2.close()
    stacks.remove(sA2)
    sW.close()
    stacks.remove(sW)
    sC.close()
    stacks.remove(sC)
    sG = push()
    NSL = 11
    wgu_sb = [sG.enter_context(nc.sbuf_tensor(f"wgu{i}", [128, NCH * 128], BF16)) for i in range(NSL)]
    sg_sb = sG.enter_context(nc.sbuf_tensor("sg_sb", [128, 4 * TB], F32))
    hf_sb = sG.enter_context(nc.sbuf_tensor("hf_sb", [128, 4 * TB], F32))
    mma_sb = sG.enter_context(nc.sbuf_tensor("mma_sb", [128, 3 * TB], F32))
    t2_sb = sG.enter_context(nc.sbuf_tensor("t2_sb", [128, TB], F32))
    msk_sb = sG.enter_context(nc.sbuf_tensor("msk_sb", [128, 4 * TB], F32))
    hsp_sb = [sG.enter_context(nc.sbuf_tensor(f"hspt{i}", [128, TB], BF16)) for i in range(2)]
    wdc_sb = [sG.enter_context(nc.sbuf_tensor(f"wdc{i}", [128, C], BF16)) for i in range(2)]

    def wd_cast(r):
        slot = wdc_sb[r % 2]
        cw = [(sem["pX1"], x1_ready)] if r == 0 else []
        if r >= 2:
            cw.append((sem["dWB"], 32 * (r - 1)))
        P.emit("gpsimd", lambda e, slot=slot, r=r: e.dma_start(
            out=slot[:], in_=wd_h[r * 128:(r + 1) * 128, :]),
            waits=cw, incs=[(sem["dWC"], 16)])
        P.emit("gpsimd", lambda e, slot=slot, r=r: e.dma_start(
            out=wd_bf[r * 128:(r + 1) * 128, :], in_=slot[:, 0:1024]),
            waits=[(sem["dWC"], 16 * (r + 1))], incs=[(sem["dWB"], 16)])
        P.emit("gpsimd", lambda e, slot=slot, r=r: e.dma_start(
            out=wd_bf[I + r * 128:I + (r + 1) * 128, :], in_=slot[:, 1024:2048]),
            incs=[(sem["dWB"], 16)])

    def gu_thresh(cid):
        G, rem = divmod(cid, 8)
        return 8 * (NTB * G + (NTB - 1)) + rem + 1

    def wgu_load(cid):
        G, rem = divmod(cid, 8)
        w, m = divmod(rem, 4)
        src = wg_h if w == 0 else wu_h
        ch = G * 4 + m
        slot = wgu_sb[cid % NSL]
        waits = [(sem["pX1"], x1_ready)] if cid == 0 else []
        if cid >= NSL:
            waits.append((sem["pGU"], gu_thresh(cid - NSL)))
        P.emit("gpsimd", lambda e, slot=slot, src=src, ch=ch: e.dma_start(
            out=slot[:], in_=src[ch * 128:(ch + 1) * 128, :]),
            waits=waits, incs=[(sem["dGU"], 16)])

    n_loaded = 0
    for _ in range(NSL):
        wgu_load(n_loaded)
        n_loaded += 1

    n_spt = 0
    A = mma_sb[:, 0:TB]
    Bm = mma_sb[:, TB:2 * TB]
    Cm = mma_sb[:, 2 * TB:3 * TB]
    for G in range(16):
        for hb in range(NTB):
            it = G * NTB + hb
            tok0 = hb * TB
            for w in range(2):
                for m in range(4):
                    cid = G * 8 + w * 4 + m
                    slot = wgu_sb[cid % NSL]
                    ps = (g_ps if w == 0 else u_ps)[m]
                    waits = [(sem["dGU"], 16 * (cid + 1))]
                    if it >= 1:
                        if w == 0:
                            waits.append((sem["aSI"], 4 * (it - 1) + m + 1))
                        else:
                            waits.append((sem["vH"], 4 * (it - 1) + m + 1))
                    elif w == 0 and m == 0:
                        waits.append((sem["pX1"], x1_ready))
                    for c in range(NCH):
                        P.emit("tensor", lambda e, ps=ps, slot=slot, c=c, hb=hb: e.matmul(
                            ps[:], slot[:, c * 128:(c + 1) * 128],
                            x1q(c, hb),
                            start=(c == 0), stop=(c == NCH - 1), skip_group_check=True),
                            waits=waits if c == 0 else [],
                            incs=[(sem["pGU"], 1)] if c == NCH - 1 else [])
                    if hb == NTB - 1 and n_loaded < 128:
                        wgu_load(n_loaded)
                        n_loaded += 1
            pg = P.count(sem["pGU"])
            for m in range(4):
                sw = [(sem["pGU"], pg - 4)] if m == 0 else []
                if it >= 1:
                    sw.append((sem["vAB"], 4 * (it - 1) + m + 1))
                P.emit("scalar", lambda e, m=m: e.activation(
                    sg_sb[:, m * TB:(m + 1) * TB], g_ps[m][:], AF.Silu),
                    waits=sw, incs=[(sem["aSI"], 1)])
            asic = P.count(sem["aSI"])
            for m in range(4):
                P.emit("vector", lambda e, m=m: e.tensor_mul(
                    hf_sb[:, m * TB:(m + 1) * TB], u_ps[m][:], sg_sb[:, m * TB:(m + 1) * TB]),
                    waits=[(sem["pGU"], pg), (sem["aSI"], asic)] if m == 0 else [],
                    incs=[(sem["vH"], 1)])
            a = [sg_sb[:, m * TB:(m + 1) * TB] for m in range(4)]
            for m in range(4):
                P.emit("vector", lambda e, m=m: e.scalar_tensor_tensor(
                    a[m], hf_sb[:, m * TB:(m + 1) * TB], c_neg1,
                    hf_sb[:, m * TB:(m + 1) * TB], OP.mult, OP.max))
            P.emit("vector", lambda e: e.tensor_max(A, a[0], a[1]))
            P.emit("vector", lambda e: e.tensor_max(Bm, a[2], a[3]))
            P.emit("vector", lambda e: e.tensor_tensor(Cm, A, Bm, OP.min))
            P.emit("vector", lambda e: e.tensor_tensor(A, a[0], a[1], OP.min))
            P.emit("vector", lambda e: e.tensor_tensor(Bm, a[2], a[3], OP.min))
            P.emit("vector", lambda e: e.tensor_max(A, A, Bm))
            P.emit("vector", lambda e: e.tensor_max(t2_sb[:], A, Cm))
            for m in range(4):
                P.emit("vector", lambda e, m=m: e.tensor_tensor(
                    msk_sb[:, m * TB:(m + 1) * TB], a[m], t2_sb[:], OP.is_ge),
                    incs=[(sem["vAB"], 1)])
            for m in range(4):
                buf = hsp_sb[n_spt % 2]
                ww = [(sem["dHW"], 16 * (n_spt - 1))] if n_spt >= 2 else []
                P.emit("vector", lambda e, m=m, buf=buf: e.tensor_mul(
                    buf[:], hf_sb[:, m * TB:(m + 1) * TB], msk_sb[:, m * TB:(m + 1) * TB]),
                    waits=ww, incs=[(sem["vSP"], 1)])
                vspc = P.count(sem["vSP"])
                row0 = hb * I + (G * 4 + m) * 128
                P.emit("sync", lambda e, buf=buf, row0=row0: e.dma_start(
                    out=hsp_dram[row0:row0 + 128, :], in_=buf[:]),
                    waits=[(sem["vSP"], vspc)], incs=[(sem["dHW"], 16)])
                n_spt += 1
        if hb == NTB - 1:
            for rr in range(4 * G, 4 * (G + 1)):
                wd_cast(rr)
    hsp_writes = P.count(sem["dHW"])

    if _DEBUG and _STAGE == 35:
        P.wait("sync", sem["dHW"], hsp_writes)
        P.emit("sync", lambda e: e.dma_start(out=dbg["hsp"], in_=hsp_dram),
               incs=[(sem["dbgs"], 16)])
    if _STAGE == 35:
        P.wait("sync", sem["dbgs"], P.count(sem["dbgs"]))
        P.wait("sync", sem["dHW"], hsp_writes)
        return finalize(nc, P, stacks)

    ppffa.close()
    stacks.remove(ppffa)
    ppffb = push()
    d_ps = [ppffb.enter_context(nc.psum_tensor(f"d_ps{i}", [128, TB], F32)) for i in range(8)]

    sG.close()
    stacks.remove(sG)
    sA.close()
    stacks.remove(sA)
    sH = push()
    NWS = 6
    wd_sb = [sH.enter_context(nc.sbuf_tensor(f"wd{i}", [128, 1024], BF16)) for i in range(NWS)]
    hst_sb = [sH.enter_context(nc.sbuf_tensor(f"hst{i}", [128, TB], BF16)) for i in range(NWS)]
    x1r_sb = [sH.enter_context(nc.sbuf_tensor(f"x1r{i}", [128, TB], F32)) for i in range(8)]
    r2nb_sb = [sH.enter_context(nc.sbuf_tensor(f"r2nb{i}", [128, NCH * TB], F32)) for i in range(2)]
    tmpb_sb = [sH.enter_context(nc.sbuf_tensor(f"tmpb{i}", [128, TB], BF16)) for i in range(2)]

    g0 = [(sem["vSP"], 256), (sem["pGU"], 8 * 64), (sem["dWB"], 32 * 64),
          (sem["dHW"], hsp_writes)]
    n_wd = 0
    n_ev2 = 0
    nfin = 0
    for nb in range(NTB):
        r2b = r2nb_sb[nb % 2]
        for dh in range(2):
            grp = 2 * nb + dh
            for k in range(8):
                d = dh * 8 + k
                x1r = x1r_sb[k]
                xw = (list(g0) if grp == 0 else [(sem["vR2"], 8 * (grp - 1) + k + 1)])
                P.emit("sync", lambda e, x1r=x1r, d=d, nb=nb: e.dma_start(
                    out=x1r[:], in_=x1_dram[d * 128:(d + 1) * 128, nb * TB:(nb + 1) * TB]),
                    waits=xw + [(sem["dX1w"], nx1w)], incs=[(sem["dX1r"], 16)])
            x1_ready = P.count(sem["dX1r"])
            for r in range(64):
                slot = wd_sb[n_wd % NWS]
                hslot = hst_sb[n_wd % NWS]
                ww = list(g0) if n_wd < NWS else [(sem["pWD"], n_wd - (NWS - 1))]
                q = "gpsimd" if (n_wd % 2 == 0) else "sync"
                wdsem = sem["dWD"] if (n_wd % 2 == 0) else sem["dWD2"]
                P.emit(q, lambda e, slot=slot, r=r, dh=dh: e.dma_start(
                    out=slot[:], in_=wd_bf[dh * I + r * 128: dh * I + (r + 1) * 128, :]),
                    waits=ww, incs=[(wdsem, 16)])
                hww = list(g0) if n_wd < NWS else [(sem["pWD"], n_wd - (NWS - 1))]
                P.emit("sync", lambda e, hslot=hslot, r=r, nb=nb: e.dma_start(
                    out=hslot[:], in_=hsp_dram[nb * I + r * 128: nb * I + (r + 1) * 128, :]),
                    waits=hww, incs=[(sem["dHR"], 16)])
                wd_ready = P.count(wdsem)
                hr_ready = P.count(sem["dHR"])
                for dc in range(8):
                    waits = [(wdsem, wd_ready), (sem["dHR"], hr_ready)]
                    if r == 0 and dc == 0 and n_ev2 > 0:
                        waits.append((sem["vR2"], n_ev2))
                    if r == 0 and dc == 0 and dh == 0 and nb >= 1:
                        waits.append((sem["vLN2"], 3 * nb - 2))
                    P.emit("tensor", lambda e, dc=dc, slot=slot, hslot=hslot, r=r: e.matmul(
                        d_ps[dc][:], slot[:, dc * 128:(dc + 1) * 128], hslot[:],
                        start=(r == 0), stop=(r == 63), skip_group_check=True),
                        waits=waits if dc == 0 else [],
                        incs=[(sem["pWD"], 1)] if dc == 7 else [])
                n_wd += 1
            for dc in range(8):
                d = dh * 8 + dc
                x1r = x1r_sb[dc]
                P.emit("vector", lambda e, r2b=r2b, d=d, dc=dc, x1r=x1r: e.tensor_add(
                    r2b[:, d * TB:(d + 1) * TB], d_ps[dc][:], x1r[:]),
                    waits=[(sem["pWD"], n_wd), (sem["dX1r"], x1_ready)],
                    incs=[(sem["vR2"], 1)])
                n_ev2 += 1

        ev_all = P.count(sem["vR2"])
        for c2 in range(NCH):
            sl = r2b[:, c2 * TB:(c2 + 1) * TB]
            tmp = tmpb_sb[c2 % 2]
            sqt = sq_sb[c2 % 2]
            cw = [(sem["vR2"], ev_all)] if c2 == 0 else (
                [(sem["pST2"], 16 * nb + c2 - 1)] if c2 >= 2 else [])
            P.emit("vector", lambda e, tmp=tmp, sl=sl: e.tensor_copy(tmp[:], sl),
                   waits=cw, incs=[(sem["pSQ2"], 1)])
            P.emit("tensor", lambda e, tmp=tmp, c2=c2: e.matmul(
                d_ps[6][0:1, 0:TB], ones_sb[:, 0:1], tmp[:],
                start=(c2 == 0), stop=(c2 == NCH - 1), skip_group_check=True),
                waits=[(sem["pSQ2"], P.count(sem["pSQ2"]))], incs=[(sem["pST2"], 1)])
            sqw = [(sem["zS2"], 16 * nb + c2 - 1)] if c2 >= 2 else []
            P.emit("vector", lambda e, sqt=sqt, tmp=tmp: e.tensor_mul(sqt[:], tmp[:], tmp[:]),
                   waits=sqw, incs=[(sem["aSQ"], 1)])
            P.emit("tensor", lambda e, sqt=sqt, c2=c2: e.matmul(
                d_ps[7][0:1, 0:TB], ones_sb[:, 0:1], sqt[:],
                start=(c2 == 0), stop=(c2 == NCH - 1), skip_group_check=True),
                waits=[(sem["aSQ"], P.count(sem["aSQ"]))], incs=[(sem["zS2"], 1)])

        l_mu = lnt_sb[0:1, 0:TB]
        l_m2 = lnt_sb[0:1, TB:2 * TB]
        l_t = lnt_sb[0:1, 2 * TB:3 * TB]
        l_r = lnt_sb[0:1, 3 * TB:4 * TB]
        stw = [(sem["pST2"], 16 * (nb + 1)), (sem["zS2"], 16 * (nb + 1))]
        if nb >= 1:
            stw.append((sem["dLN2"], 64 * (nb - 1) + 32))
        P.emit("vector", lambda e: e.tensor_scalar(l_mu, d_ps[6][0:1, 0:TB], c_i2048, None, OP.mult),
               waits=stw)
        P.emit("vector", lambda e: e.tensor_scalar(l_m2, d_ps[7][0:1, 0:TB], c_i2048, None, OP.mult))
        P.emit("vector", lambda e: e.tensor_mul(l_t, l_mu, l_mu))
        P.emit("vector", lambda e: e.tensor_sub(l_r, l_m2, l_t), incs=[(sem["vLN2"], 1)])
        vlc = P.count(sem["vLN2"])
        P.emit("scalar", lambda e: e.activation(l_t, l_r, AF.Sqrt, bias=c_eps),
               waits=[(sem["vLN2"], vlc)], incs=[(sem["vLN2"], 1)])
        vlc = P.count(sem["vLN2"])
        P.emit("vector", lambda e: e.reciprocal(l_r, l_t), waits=[(sem["vLN2"], vlc)])
        P.emit("vector", lambda e: e.tensor_mul(l_m2, l_mu, l_r), incs=[(sem["vLN2"], 1)])
        vlc = P.count(sem["vLN2"])
        P.emit("sync", lambda e, nb=nb: e.dma_start(
            out=ln_dram[0:1, nb * TB:(nb + 1) * TB], in_=l_r),
            waits=[(sem["vLN2"], vlc)], incs=[(sem["dLN2"], 16)])
        P.emit("sync", lambda e, nb=nb: e.dma_start(
            out=ln_dram[1:2, nb * TB:(nb + 1) * TB], in_=l_m2),
            incs=[(sem["dLN2"], 16)])
        dlc = P.count(sem["dLN2"])
        P.emit("sync", lambda e, nb=nb: e.dma_start(
            out=rstd_b[:, nb * TB:(nb + 1) * TB],
            in_=ln_dram[0:1, nb * TB:(nb + 1) * TB].partition_broadcast(128)),
            waits=[(sem["dLN2"], dlc)], incs=[(sem["dLN2"], 16)])
        P.emit("sync", lambda e, nb=nb: e.dma_start(
            out=nu_b[:, nb * TB:(nb + 1) * TB],
            in_=ln_dram[1:2, nb * TB:(nb + 1) * TB].partition_broadcast(128)),
            incs=[(sem["dLN2"], 16)])
        dl_all = P.count(sem["dLN2"])

        for c2 in range(NCH):
            xm = x1m_sb[nfin % 2]
            xv = x1t_sb[nfin % 2]
            fw = [(sem["dLN2"], dl_all)] if c2 == 0 else []
            P.emit("vector", lambda e, xm=xm, r2b=r2b, c2=c2, nb=nb: e.tensor_mul(
                xm[:], r2b[:, c2 * TB:(c2 + 1) * TB], rstd_b[:, nb * TB:(nb + 1) * TB]),
                waits=fw)
            ow = [(sem["dOUT"], 16 * (nfin - 1))] if nfin >= 2 else []
            P.emit("vector", lambda e, xv=xv, xm=xm, nb=nb: e.tensor_sub(
                xv[:], xm[:], nu_b[:, nb * TB:(nb + 1) * TB]),
                waits=ow, incs=[(sem["vF"], 1)])
            vfc = P.count(sem["vF"])
            P.emit("sync", lambda e, xv=xv, c2=c2, nb=nb: e.dma_start(
                out=out_fm[c2 * 128:(c2 + 1) * 128, nb * TB:(nb + 1) * TB], in_=xv[:]),
                waits=[(sem["vF"], vfc)], incs=[(sem["dOUT"], 16)])
            nfin += 1

    P.wait("sync", sem["dOUT"], 16 * nfin)
    return finalize(nc, P, stacks)


def finalize(nc, P, stacks):
    with nc.Block() as block:
        @block.tensor
        def _(e):
            P.replay("tensor", e)

        @block.vector
        def _(e):
            P.replay("vector", e)

        @block.scalar
        def _(e):
            P.replay("scalar", e)

        @block.gpsimd
        def _(e):
            P.replay("gpsimd", e)

        @block.sync
        def _(e):
            P.replay("sync", e)
    for s in reversed(stacks):
        try:
            s.close()
        except Exception:
            pass
    return nc


def _host_prep(x, Wq, Wk, Wv, Wo, wgm, wum, wdm):
    x = np.asarray(x, dtype=np.float32)
    Wq = np.asarray(Wq, np.float32); Wk = np.asarray(Wk, np.float32)
    Wv = np.asarray(Wv, np.float32); Wo = np.asarray(Wo, np.float32)
    wgm = np.asarray(wgm, np.float32); wum = np.asarray(wum, np.float32)
    wdm = np.asarray(wdm, np.float32)

    inv = (1.0 / 10000.0 ** (np.arange(0, D, 2, dtype=np.float32) / np.float32(D))).astype(np.float32)
    f = np.arange(T, dtype=np.float32)[:, None] * inv[None, :]
    cos_t = np.ascontiguousarray(np.concatenate([np.cos(f)] * 2, axis=1).T)
    sin_t = np.ascontiguousarray(np.concatenate([np.sin(f)] * 2, axis=1).T)

    masks = np.zeros((128, 4 * QT), dtype=ml_dtypes.bfloat16)
    for o in range(4):
        kk = 128 * o + np.arange(128)[:, None]
        qq = np.arange(QT)[None, :]
        masks[:, o * QT:(o + 1) * QT] = (kk <= qq).astype(ml_dtypes.bfloat16)

    ones_bf = np.ones((128, 1), dtype=ml_dtypes.bfloat16)
    cvec = np.zeros((128, 8), dtype=np.float32)
    cvec[:, 0] = EXP_SHIFT
    cvec[:, 1] = 1e-5
    cvec[:, 2] = 1.0 / 2048.0
    cvec[:, 3] = -1.0
    cvec[:, 4] = 1.0

    Gi, mi, pi = np.meshgrid(np.arange(16), np.arange(4), np.arange(128), indexing="ij")
    idx = (4 * (128 * Gi + pi) + mi).reshape(-1)
    wgp = wgm[:, idx]
    wup = wum[:, idx]
    wd_p = np.ascontiguousarray(wdm[idx, :])
    wg_h = np.ascontiguousarray(
        wgp.reshape(NCH, 128, 64, 128).transpose(2, 1, 0, 3).reshape(64 * 128, NCH * 128))
    wu_h = np.ascontiguousarray(
        wup.reshape(NCH, 128, 64, 128).transpose(2, 1, 0, 3).reshape(64 * 128, NCH * 128))
    wo_h = np.ascontiguousarray(
        Wo.reshape(NCH, 128, NCH, 128).transpose(1, 2, 0, 3).reshape(128, NCH * C))

    xtl = [np.ascontiguousarray(
        x[b].reshape(NT, QT, NCH, 128).transpose(0, 3, 2, 1).reshape(NT * 128, NCH * QT))
        for b in range(B)]

    in_maps = []
    for c in range(8):
        b, g = c // 4, c % 4
        wq_c = Wq[:, 512 * g:512 * (g + 1)]
        wq_h = np.ascontiguousarray(
            wq_c.reshape(NCH, 128, 512).transpose(1, 0, 2).reshape(128, NCH * 512))
        wk_c = Wk[:, 128 * g:128 * (g + 1)]
        wk_hh = np.ascontiguousarray(
            wk_c.reshape(NCH, 128, 128).transpose(1, 0, 2).reshape(128, NCH * 128))
        wv_c = Wv[:, 128 * g:128 * (g + 1)]
        wv_hh = np.ascontiguousarray(
            wv_c.reshape(NCH, 128, 128).transpose(1, 0, 2).reshape(128, NCH * 128))
        toks = np.concatenate([np.arange(qq * TPQ + g * TB, qq * TPQ + (g + 1) * TB)
                               for qq in range(4)])
        in_maps.append({
            "x_tiles": xtl[b],
            "x_res": np.ascontiguousarray(x[b, toks, :].T),
            "wq_h": wq_h, "wk_h": wk_hh, "wv_h": wv_hh,
            "wo_h": wo_h, "wg_h": wg_h, "wu_h": wu_h, "wd_h": wd_p,
            "cos_t": cos_t, "sin_t": sin_t, "masks": masks,
            "ones_bf": ones_bf, "cvec": cvec,
        })
    return in_maps


def kernel(L0_attn_x, L0_attn_Wq, L0_attn_Wk, L0_attn_Wv, L0_attn_Wo,
           L0_ffn_w_gate, L0_ffn_w_up, L0_ffn_w_down):
    in_maps = _host_prep(L0_attn_x, L0_attn_Wq, L0_attn_Wk, L0_attn_Wv, L0_attn_Wo,
                         L0_ffn_w_gate, L0_ffn_w_up, L0_ffn_w_down)
    nc = build_program()
    res = run_bass_kernel_spmd(nc, in_maps, core_ids=list(range(8)))
    out = np.empty((B, T, C), dtype=np.float32)
    for c in range(8):
        b, g = c // 4, c % 4
        o = np.asarray(res.results[c]["out_fm"]).T
        for qq in range(4):
            out[b, qq * TPQ + g * TB: qq * TPQ + (g + 1) * TB, :] = o[qq * TB:(qq + 1) * TB, :]
    return out


# version 19
# speedup vs baseline: 1.0121x; 1.0121x over previous
import dataclasses
import contextlib
import numpy as np
import ml_dtypes
import concourse.bass as bass
import concourse.mybir as mybir
from concourse.bass_utils import run_bass_kernel_spmd

F32 = mybir.dt.float32
BF16 = mybir.dt.bfloat16
AF = mybir.ActivationFunctionType
OP = mybir.AluOpType

B, T, C = 2, 8192, 2048
HPC = 4
D = 128
I = 8192
QT = 512
NT = T // QT
TPQ = 2048
TB = 512
NTB = TPQ // TB
NCH = C // 128
EXP_SHIFT = -12.0
EXP_SCALE = float(1.0 / np.sqrt(np.float64(D)))

_STAGE = 4
_DEBUG = False


class Prog:
    ENGINES = ("tensor", "vector", "scalar", "gpsimd", "sync")

    def __init__(self):
        self.ops = {e: [] for e in self.ENGINES}
        self.cnt = {}

    def count(self, sem):
        return self.cnt.get(sem.name, 0)

    def emit(self, eng, build, waits=(), incs=()):
        incs = tuple(incs)
        for s, n in incs:
            self.cnt[s.name] = self.cnt.get(s.name, 0) + n
        self.ops[eng].append((build, tuple(waits), incs))

    def wait(self, eng, s, v):
        self.ops[eng].append((None, ((s, v),), ()))

    def replay(self, name, eng):
        for build, waits, incs in self.ops[name]:
            for s, v in waits:
                if v > 0:
                    eng.wait_ge(s, v)
            if build is None:
                continue
            ins = build(eng)
            for s, n in incs:
                ins = ins.then_inc(s, n)


def build_program():
    nc = bass.Bass("TRN2", target_bir_lowering=False, debug=False)

    x_tiles = nc.dram_tensor("x_tiles", [NT * 128, NCH * QT], F32, kind="ExternalInput").ap()
    x_res = nc.dram_tensor("x_res", [C, TPQ], F32, kind="ExternalInput").ap()
    wq_h = nc.dram_tensor("wq_h", [128, NCH * HPC * D], F32, kind="ExternalInput").ap()
    wk_h = nc.dram_tensor("wk_h", [128, NCH * D], F32, kind="ExternalInput").ap()
    wv_h = nc.dram_tensor("wv_h", [128, NCH * D], F32, kind="ExternalInput").ap()
    wo_h = nc.dram_tensor("wo_h", [128, NCH * C], F32, kind="ExternalInput").ap()
    wg_h = nc.dram_tensor("wg_h", [64 * 128, NCH * 128], F32, kind="ExternalInput").ap()
    wu_h = nc.dram_tensor("wu_h", [64 * 128, NCH * 128], F32, kind="ExternalInput").ap()
    wd_h = nc.dram_tensor("wd_h", [I, C], F32, kind="ExternalInput").ap()
    cos_in = nc.dram_tensor("cos_t", [D, T], F32, kind="ExternalInput").ap()
    sin_in = nc.dram_tensor("sin_t", [D, T], F32, kind="ExternalInput").ap()
    masks_in = nc.dram_tensor("masks", [128, 4 * QT], BF16, kind="ExternalInput").ap()
    ones_in = nc.dram_tensor("ones_bf", [128, 1], BF16, kind="ExternalInput").ap()
    cvec_in = nc.dram_tensor("cvec", [128, 8], F32, kind="ExternalInput").ap()

    out_fm = nc.dram_tensor("out_fm", [C, TPQ], F32, kind="ExternalOutput").ap()

    contrib = [nc.dram_tensor(f"contrib{q}", [HPC * D, TPQ], BF16, kind="Internal").ap()
               for q in range(4)]
    gathered = nc.dram_tensor("gathered", [4 * C, TPQ], BF16, kind="Internal").ap()
    zscr = nc.dram_tensor("zscr", [NT * HPC, QT], F32, kind="Internal").ap()
    r1_dram = nc.dram_tensor("r1_dram", [C, TPQ], F32, kind="Internal").ap()
    x1_dram = nc.dram_tensor("x1_dram", [C, TPQ], F32, kind="Internal").ap()
    hsp_dram = nc.dram_tensor("hsp_dram", [NTB * I, TB], BF16, kind="Internal").ap()
    wd_bf = nc.dram_tensor("wd_bf", [2 * I, 1024], BF16, kind="Internal").ap()
    ln_dram = nc.dram_tensor("ln_dram", [2, 2 * TPQ], F32, kind="Internal").ap()

    dbg = {}
    if _DEBUG:
        if _STAGE == 1:
            dbg["q"] = nc.dram_tensor("dbg_q", [128, HPC * T], BF16, kind="ExternalOutput").ap()
            dbg["k"] = nc.dram_tensor("dbg_k", [128, T], BF16, kind="ExternalOutput").ap()
            dbg["v"] = nc.dram_tensor("dbg_v", [128, T], BF16, kind="ExternalOutput").ap()
        if _STAGE == 2:
            dbg["gat"] = nc.dram_tensor("dbg_gat", [4 * C, TPQ], BF16, kind="ExternalOutput").ap()
        if _STAGE == 3:
            dbg["x1"] = nc.dram_tensor("dbg_x1", [C, TPQ], F32, kind="ExternalOutput").ap()
        if _STAGE == 35:
            dbg["hsp"] = nc.dram_tensor("dbg_hsp", [I, TPQ], BF16, kind="ExternalOutput").ap()
        if _STAGE == 37:
            dbg["r2"] = nc.dram_tensor("dbg_r2", [C, TPQ], F32, kind="ExternalOutput").ap()

    P = Prog()
    stacks = []

    def push():
        s = contextlib.ExitStack()
        stacks.append(s)
        return s

    st = push()
    sem = {}
    for name in ("dMC", "dW1", "dXT", "dCS", "pQ", "pK", "pV", "vQ", "vK", "vV",
                 "xtF", "csF", "za", "zm",
                 "sp", "xp", "mk", "pv", "rc", "ev", "dZ", "dCT", "cc",
                 "dWO", "dGT", "dXR", "pWO", "vR1", "dR1w", "pR1",
                 "pST1", "pSQ1", "vLN1", "dLN1", "dR1r", "vX1", "dX1w", "pX1",
                 "dGU", "pGU", "aSI", "vH", "vAB", "vSP", "dHW",
                 "dHR", "dWD", "pWD", "pFB", "vR2", "dWC", "dWB", "dR2w", "pR2", "dX1r",
                 "pST2", "pSQ2", "vLN2", "dLN2", "dR2r", "dOUT", "dbgs",
                 "aSQ", "zS2", "vF", "dWD2", "sS2", "ccd"):
        sem[name] = st.enter_context(nc.semaphore(name))

    ones_sb = st.enter_context(nc.sbuf_tensor("ones_sb", [128, 1], BF16))
    cvec_sb = st.enter_context(nc.sbuf_tensor("cvec_sb", [128, 8], F32))

    P.emit("sync", lambda e: e.dma_start(out=ones_sb[:], in_=ones_in), incs=[(sem["dMC"], 16)])
    P.emit("sync", lambda e: e.dma_start(out=cvec_sb[:], in_=cvec_in), incs=[(sem["dMC"], 16)])
    consts_ready = P.count(sem["dMC"])

    c_exp = cvec_sb[:, 0:1]
    c_eps = cvec_sb[0:1, 1:2]
    c_i2048 = cvec_sb[0:1, 2:3]
    c_neg1 = cvec_sb[:, 3:4]
    c_ones32 = cvec_sb[:, 4:5]

    sA = push()
    q_all = sA.enter_context(nc.sbuf_tensor("q_all", [128, HPC * T], BF16, side="right"))
    sA2 = push()
    k_all = sA2.enter_context(nc.sbuf_tensor("k_all", [128, T], BF16, side="right"))
    v_all = sA2.enter_context(nc.sbuf_tensor("v_all", [128, T], BF16, side="right"))

    sB = push()
    wq_sb = sB.enter_context(nc.sbuf_tensor("wq_sb", [128, NCH * HPC * D], BF16))
    wk_sb = sB.enter_context(nc.sbuf_tensor("wk_sb", [128, NCH * D], BF16))
    wv_sb = sB.enter_context(nc.sbuf_tensor("wv_sb", [128, NCH * D], BF16))
    xt = [sB.enter_context(nc.sbuf_tensor(f"xt{i}", [128, NCH * QT], BF16)) for i in range(2)]
    cs = [sB.enter_context(nc.sbuf_tensor(f"cs{i}", [128, 2 * QT], F32)) for i in range(2)]
    rtmp1 = sB.enter_context(nc.sbuf_tensor("rtmp1", [128, QT], F32))
    rtmp2 = sB.enter_context(nc.sbuf_tensor("rtmp2", [128, QT], F32))

    pp1 = push()
    q_ps = [pp1.enter_context(nc.psum_tensor(f"q_ps{i}", [128, QT], F32)) for i in range(2)]
    k_ps = [pp1.enter_context(nc.psum_tensor(f"k_ps{i}", [128, QT], F32)) for i in range(2)]
    v_ps = [pp1.enter_context(nc.psum_tensor(f"v_ps{i}", [128, 128], F32)) for i in range(2)]

    P.emit("gpsimd", lambda e: e.dma_start(out=wq_sb[:], in_=wq_h), incs=[(sem["dW1"], 16)])
    P.emit("gpsimd", lambda e: e.dma_start(out=wk_sb[:], in_=wk_h), incs=[(sem["dW1"], 16)])
    P.emit("gpsimd", lambda e: e.dma_start(out=wv_sb[:], in_=wv_h), incs=[(sem["dW1"], 16)])
    w1_ready = P.count(sem["dW1"])

    def rope_emit(ps, dst, tau, psem, pval, vsem):
        cst = cs[tau % 2]
        cs_ready = 32 * (tau + 1)
        P.emit("vector", lambda e: e.tensor_mul(rtmp1[:], ps[:], cst[:, 0:QT]),
               waits=[(sem[psem], pval), (sem["dCS"], cs_ready)])
        P.emit("vector", lambda e: e.tensor_mul(rtmp2[0:64, :], ps[64:128, :], cst[0:64, QT:2 * QT]))
        P.emit("vector", lambda e: e.tensor_mul(rtmp2[64:128, :], ps[0:64, :], cst[64:128, QT:2 * QT]))
        P.emit("vector", lambda e: e.tensor_sub(dst[0:64, :], rtmp1[0:64, :], rtmp2[0:64, :]))
        P.emit("vector", lambda e: e.tensor_add(dst[64:128, :], rtmp1[64:128, :], rtmp2[64:128, :]),
               incs=[(sem[vsem], 1)])

    for tau in range(NT):
        xb = xt[tau % 2]
        t0 = QT * tau
        xw = [(sem["pV"], 4 * (tau - 1))] if tau >= 2 else []
        P.emit("gpsimd", lambda e, xb=xb, tau=tau: e.dma_start(
            out=xb[:], in_=x_tiles[tau * 128:(tau + 1) * 128, :]),
            waits=xw, incs=[(sem["dXT"], 16)])
        xt_ready = P.count(sem["dXT"])
        cw = [(sem["vK"], tau - 1)] if tau >= 2 else []
        csb = cs[tau % 2]
        P.emit("sync", lambda e, csb=csb, t0=t0: e.dma_start(out=csb[:, 0:QT], in_=cos_in[:, t0:t0 + QT]),
               waits=cw, incs=[(sem["dCS"], 16)])
        P.emit("sync", lambda e, csb=csb, t0=t0: e.dma_start(out=csb[:, QT:2 * QT], in_=sin_in[:, t0:t0 + QT]),
               incs=[(sem["dCS"], 16)])

        for h in range(HPC):
            ps = q_ps[h % 2]
            idx = tau * HPC + h
            pw = [(sem["dW1"], w1_ready), (sem["dXT"], xt_ready)]
            if idx >= 2:
                pw.append((sem["vQ"], idx - 1))
            for c in range(NCH):
                P.emit("tensor", lambda e, ps=ps, c=c, h=h, xb=xb: e.matmul(
                    ps[:], wq_sb[:, c * HPC * D + h * D: c * HPC * D + (h + 1) * D],
                    xb[:, c * QT:(c + 1) * QT],
                    start=(c == 0), stop=(c == NCH - 1), skip_group_check=True),
                    waits=pw if c == 0 else [],
                    incs=[(sem["pQ"], 1)] if c == NCH - 1 else [])
            rope_emit(ps, q_all[:, h * T + t0: h * T + t0 + QT], tau, "pQ", P.count(sem["pQ"]), "vQ")

        ps = k_ps[tau % 2]
        pw = [(sem["dW1"], w1_ready), (sem["dXT"], xt_ready)]
        if tau >= 2:
            pw.append((sem["vK"], tau - 1))
        for c in range(NCH):
            P.emit("tensor", lambda e, ps=ps, c=c, xb=xb: e.matmul(
                ps[:], wk_sb[:, c * D:(c + 1) * D], xb[:, c * QT:(c + 1) * QT],
                start=(c == 0), stop=(c == NCH - 1), skip_group_check=True),
                waits=pw if c == 0 else [],
                incs=[(sem["pK"], 1)] if c == NCH - 1 else [])
        rope_emit(ps, k_all[:, t0:t0 + QT], tau, "pK", P.count(sem["pK"]), "vK")

        for s in range(4):
            ps = v_ps[s % 2]
            idx = tau * 4 + s
            pw = [(sem["dW1"], w1_ready), (sem["dXT"], xt_ready)]
            if idx >= 2:
                pw.append((sem["vV"], idx - 1))
            for c in range(NCH):
                P.emit("tensor", lambda e, ps=ps, c=c, s=s, xb=xb: e.matmul(
                    ps[:], xb[:, c * QT + s * 128: c * QT + (s + 1) * 128],
                    wv_sb[:, c * D:(c + 1) * D],
                    start=(c == 0), stop=(c == NCH - 1), skip_group_check=True),
                    waits=pw if c == 0 else [],
                    incs=[(sem["pV"], 1)] if c == NCH - 1 else [])
            P.emit("vector", lambda e, ps=ps, t0=t0, s=s: e.tensor_copy(
                v_all[:, t0 + s * 128: t0 + (s + 1) * 128], ps[:]),
                waits=[(sem["pV"], P.count(sem["pV"]))],
                incs=[(sem["vV"], 1)])

    qkv_done = (P.count(sem["vQ"]), P.count(sem["vK"]), P.count(sem["vV"]))

    if _DEBUG and _STAGE == 1:
        P.emit("sync", lambda e: e.dma_start(out=dbg["q"], in_=q_all[:]),
               waits=[(sem["vQ"], qkv_done[0])], incs=[(sem["dbgs"], 16)])
        P.emit("sync", lambda e: e.dma_start(out=dbg["k"], in_=k_all[:]),
               waits=[(sem["vK"], qkv_done[1])], incs=[(sem["dbgs"], 16)])
        P.emit("sync", lambda e: e.dma_start(out=dbg["v"], in_=v_all[:]),
               waits=[(sem["vV"], qkv_done[2])], incs=[(sem["dbgs"], 16)])
    if _STAGE == 1:
        P.wait("sync", sem["dbgs"], P.count(sem["dbgs"]))
        P.wait("vector", sem["vV"], qkv_done[2])
        return finalize(nc, P, stacks)

    pp1.close()
    stacks.remove(pp1)
    pp2 = push()
    s_ps = [pp2.enter_context(nc.psum_tensor(f"s_ps{i}", [128, 2 * QT], F32)) for i in range(2)]
    o_ps = [pp2.enter_context(nc.psum_tensor(f"o_ps{i}", [128, QT], F32)) for i in range(4)]
    z_ps = [s_ps[0][0:1, 0:QT], s_ps[0][0:1, QT:2 * QT]]

    sB.close()
    stacks.remove(sB)
    sF = push()
    sq_sb = [sF.enter_context(nc.sbuf_tensor(f"sq{i}", [128, TB], BF16)) for i in range(2)]
    lnt_sb = sF.enter_context(nc.sbuf_tensor("lnt_sb", [1, 4 * TB], F32))
    rstd_b = sF.enter_context(nc.sbuf_tensor("rstd_b", [128, TPQ], F32))
    nu_b = sF.enter_context(nc.sbuf_tensor("nu_b", [128, TPQ], F32))
    x1t_sb = [sF.enter_context(nc.sbuf_tensor(f"x1t{i}", [128, TB], F32)) for i in range(2)]
    x1m_sb = [sF.enter_context(nc.sbuf_tensor(f"x1m{i}", [128, TB], F32)) for i in range(2)]
    r1t_sb = [sF.enter_context(nc.sbuf_tensor(f"r1t{i}", [128, TB], F32)) for i in range(2)]

    sC = push()
    masks_sb = sC.enter_context(nc.sbuf_tensor("masks_sb", [128, 4 * QT], BF16))
    p_sb = [sC.enter_context(nc.sbuf_tensor(f"p_sb{i}", [128, 2 * QT], BF16)) for i in range(2)]
    zacc_sb = [sC.enter_context(nc.sbuf_tensor(f"zacc{i}", [128, 2 * QT], F32)) for i in range(2)]
    zinv_sb = [sC.enter_context(nc.sbuf_tensor(f"zinv{i}", [1, QT], F32)) for i in range(2)]
    zb_sb = [sC.enter_context(nc.sbuf_tensor(f"zb{i}", [128, QT], F32)) for i in range(2)]
    oo_sb = [sC.enter_context(nc.sbuf_tensor(f"oo{i}", [128, QT], BF16)) for i in range(2)]
    ccd_sb = sC.enter_context(nc.sbuf_tensor("ccd_sb", [128, 1], BF16))
    gat_sb = sC.enter_context(nc.sbuf_tensor("gat_sb", [128, NCH * TB], BF16))
    sqq_sb = sC.enter_context(nc.sbuf_tensor("sqq_sb", [128, NCH * TB], BF16))
    xr_sb = [sC.enter_context(nc.sbuf_tensor(f"xr{i}", [128, TB], F32)) for i in range(2)]
    tbq_sb = [sC.enter_context(nc.sbuf_tensor(f"tbq{i}", [128, TB], BF16)) for i in range(2)]

    P.emit("sync", lambda e: e.dma_start(out=masks_sb[:], in_=masks_in),
           waits=[(sem["vV"], qkv_done[2])], incs=[(sem["dMC"], 16)])
    consts_ready = P.count(sem["dMC"])

    sW = push()
    wo_sb = [sW.enter_context(nc.sbuf_tensor(f"wo_sb{i}", [128, NCH * 128], BF16))
             for i in range(2)]

    P.wait("tensor", sem["vQ"], qkv_done[0])
    P.wait("tensor", sem["vK"], qkv_done[1])
    P.wait("tensor", sem["vV"], qkv_done[2])
    P.wait("tensor", sem["dMC"], 32)
    P.wait("scalar", sem["dMC"], consts_ready)
    P.wait("vector", sem["dMC"], consts_ready)

    pid = nc.partition_id([mybir.EngineType.SP])
    grp_off = (pid % 4) * TB

    def x1q(c, hb):
        o = (c // 4) * T + hb * TPQ + (c % 4) * TB
        return q_all[:, o:o + TB]

    def emit_post(qq):
        gw = [(sem["cc"], 4), (sem["ccd"], 4)]
        if qq >= 1:
            gw.append((sem["pWO"], 256 * qq))
        src0 = gathered[0:128, 0:TB]
        src3 = dataclasses.replace(
            src0, offset=src0.offset + qq * (C * TPQ) + grp_off,
            ap=mybir.VecI64Pair([[TPQ, 128], [128 * TPQ, NCH], [1, TB]]))
        gb3 = gat_sb[:].rearrange("p (c n) -> p c n", c=NCH)
        P.emit("sync", lambda e, gb3=gb3, src3=src3: e.dma_start(out=gb3, in_=src3),
               waits=gw, incs=[(sem["dGT"], 16)])
        gt_ready = P.count(sem["dGT"])
        for c2 in range(NCH):
            ngrp = qq * NCH + c2
            wslot = wo_sb[ngrp % 2]
            loww = [(sem["pWO"], 16 * (ngrp - 1))] if ngrp >= 2 else []
            P.emit("gpsimd", lambda e, wslot=wslot, c2=c2: e.dma_start(
                out=wslot[:], in_=wo_h[:, c2 * C:(c2 + 1) * C]),
                waits=loww, incs=[(sem["dWO"], 16)])
            ps = wo_ps
            mw = [(sem["dWO"], P.count(sem["dWO"])), (sem["dGT"], gt_ready)]
            if ngrp >= 1:
                mw.append((sem["vR1"], ngrp))
            for c1 in range(NCH):
                P.emit("tensor", lambda e, c1=c1, wslot=wslot, ps=ps: e.matmul(
                    ps[:], wslot[:, c1 * 128:(c1 + 1) * 128],
                    gat_sb[:, c1 * TB:(c1 + 1) * TB],
                    start=(c1 == 0), stop=(c1 == NCH - 1), skip_group_check=True),
                    waits=mw if c1 == 0 else [],
                    incs=[(sem["pWO"], 1)])
            xrb = xr_sb[ngrp % 2]
            xw = [(sem["vR1"], ngrp - 1)] if ngrp >= 2 else []
            P.emit("gpsimd", lambda e, xrb=xrb, c2=c2, qq=qq: e.dma_start(
                out=xrb[:], in_=x_res[c2 * 128:(c2 + 1) * 128, qq * TB:(qq + 1) * TB]),
                waits=xw, incs=[(sem["dXR"], 16)])
            rt = r1t_sb[ngrp % 2]
            ww = [(sem["pWO"], 16 * (ngrp + 1)), (sem["dXR"], P.count(sem["dXR"]))]
            if ngrp >= 2:
                ww.append((sem["dR1w"], 16 * (ngrp - 1)))
                ww.append((sem["pR1"], ngrp - 1))
                ww.append((sem["vX1"], 16 * qq))
            P.emit("vector", lambda e, rt=rt, xrb=xrb, ps=ps: e.tensor_add(rt[:], ps[:], xrb[:]),
                   waits=ww, incs=[(sem["vR1"], 1)])
            vr = P.count(sem["vR1"])
            P.emit("sync", lambda e, rt=rt, c2=c2, qq=qq: e.dma_start(
                out=r1_dram[c2 * 128:(c2 + 1) * 128, qq * TB:(qq + 1) * TB], in_=rt[:]),
                waits=[(sem["vR1"], vr)], incs=[(sem["dR1w"], 16)])
            tb16 = tbq_sb[ngrp % 2]
            cw = [(sem["pST1"], ngrp - 1)] if ngrp >= 2 else []
            P.emit("vector", lambda e, tb16=tb16, rt=rt: e.tensor_copy(tb16[:], rt[:]),
                   waits=cw, incs=[(sem["pR1"], 1)])
            P.emit("vector", lambda e, tb16=tb16, c2=c2: e.tensor_mul(
                sqq_sb[:, c2 * TB:(c2 + 1) * TB], tb16[:], tb16[:]),
                incs=[(sem["pSQ1"], 1)])
            sw1 = [(sem["pR1"], P.count(sem["pR1"]))]
            if c2 == 0 and qq >= 1:
                sw1.append((sem["vLN1"], 5 * qq))
            P.emit("tensor", lambda e, tb16=tb16, c2=c2: e.matmul(
                st_ps[0:1, 0:TB], ones_sb[:, 0:1], tb16[:],
                start=(c2 == 0), stop=(c2 == NCH - 1), skip_group_check=True),
                waits=sw1, incs=[(sem["pST1"], 1)])
        l_mu = lnt_sb[0:1, 0:TB]
        l_m2 = lnt_sb[0:1, TB:2 * TB]
        l_t = lnt_sb[0:1, 2 * TB:3 * TB]
        l_r = lnt_sb[0:1, 3 * TB:4 * TB]
        stw = [(sem["pST1"], 16 * (qq + 1))]
        if qq >= 1:
            stw.append((sem["dLN1"], 64 * (qq - 1) + 32))
        P.emit("vector", lambda e: e.tensor_scalar(l_mu, st_ps[0:1, 0:TB], c_i2048, None, OP.mult),
               waits=stw, incs=[(sem["vLN1"], 1)])
        for c2 in range(NCH):
            s2w = ([(sem["vLN1"], 5 * qq + 1), (sem["pSQ1"], 16 * (qq + 1))]
                   if c2 == 0 else [])
            P.emit("tensor", lambda e, c2=c2: e.matmul(
                st_ps[0:1, 0:TB], ones_sb[:, 0:1], sqq_sb[:, c2 * TB:(c2 + 1) * TB],
                start=(c2 == 0), stop=(c2 == NCH - 1), skip_group_check=True),
                waits=s2w, incs=[(sem["sS2"], 1)])
        P.emit("vector", lambda e: e.tensor_scalar(l_m2, st_ps[0:1, 0:TB], c_i2048, None, OP.mult),
               waits=[(sem["sS2"], 16 * (qq + 1))], incs=[(sem["vLN1"], 1)])
        P.emit("vector", lambda e: e.tensor_mul(l_t, l_mu, l_mu))
        P.emit("vector", lambda e: e.tensor_sub(l_r, l_m2, l_t), incs=[(sem["vLN1"], 1)])
        P.emit("scalar", lambda e: e.activation(l_t, l_r, AF.Sqrt, bias=c_eps),
               waits=[(sem["vLN1"], 5 * qq + 3)], incs=[(sem["vLN1"], 1)])
        P.emit("vector", lambda e: e.reciprocal(l_r, l_t), waits=[(sem["vLN1"], 5 * qq + 4)])
        P.emit("vector", lambda e: e.tensor_mul(l_m2, l_mu, l_r), incs=[(sem["vLN1"], 1)])
        P.emit("sync", lambda e, qq=qq: e.dma_start(
            out=ln_dram[0:1, qq * TB:(qq + 1) * TB], in_=l_r),
            waits=[(sem["vLN1"], 5 * qq + 5)], incs=[(sem["dLN1"], 16)])
        P.emit("sync", lambda e, qq=qq: e.dma_start(
            out=ln_dram[1:2, qq * TB:(qq + 1) * TB], in_=l_m2),
            incs=[(sem["dLN1"], 16)])
        dlc = P.count(sem["dLN1"])
        P.emit("sync", lambda e, qq=qq: e.dma_start(
            out=rstd_b[:, qq * TB:(qq + 1) * TB],
            in_=ln_dram[0:1, qq * TB:(qq + 1) * TB].partition_broadcast(128)),
            waits=[(sem["dLN1"], dlc)], incs=[(sem["dLN1"], 16)])
        P.emit("sync", lambda e, qq=qq: e.dma_start(
            out=nu_b[:, qq * TB:(qq + 1) * TB],
            in_=ln_dram[1:2, qq * TB:(qq + 1) * TB].partition_broadcast(128)),
            incs=[(sem["dLN1"], 16)])
        dl1 = P.count(sem["dLN1"])
        for c2 in range(NCH):
            k = qq * NCH + c2
            rt = r1t_sb[k % 2]
            rw = [(sem["dLN1"], dl1), (sem["dR1w"], 256 * (qq + 1)),
                  (sem["pR1"], 16 * (qq + 1))]
            if k >= 2:
                rw.append((sem["vX1"], k - 1))
            P.emit("sync", lambda e, rt=rt, c2=c2, qq=qq: e.dma_start(
                out=rt[:], in_=r1_dram[c2 * 128:(c2 + 1) * 128, qq * TB:(qq + 1) * TB]),
                waits=rw, incs=[(sem["dR1r"], 16)])
            drc = P.count(sem["dR1r"])
            xm = x1m_sb[k % 2]
            xv = x1t_sb[k % 2]
            P.emit("vector", lambda e, xm=xm, rt=rt, qq=qq: e.tensor_mul(
                xm[:], rt[:], rstd_b[:, qq * TB:(qq + 1) * TB]),
                waits=[(sem["dR1r"], drc)])
            xw2 = []
            if k >= 2:
                xw2 = [(sem["dX1w"], 16 * (k - 1)), (sem["pX1"], k - 1)]
            P.emit("vector", lambda e, xv=xv, xm=xm, qq=qq: e.tensor_sub(
                xv[:], xm[:], nu_b[:, qq * TB:(qq + 1) * TB]),
                waits=xw2, incs=[(sem["vX1"], 1)])
            vxc = P.count(sem["vX1"])
            P.emit("sync", lambda e, xv=xv, c2=c2, qq=qq: e.dma_start(
                out=x1_dram[c2 * 128:(c2 + 1) * 128, qq * TB:(qq + 1) * TB], in_=xv[:]),
                waits=[(sem["vX1"], vxc)], incs=[(sem["dX1w"], 16)])
            P.emit("gpsimd", lambda e, xv=xv, c2=c2, qq=qq: e.tensor_copy(
                x1q(c2, qq), xv[:]),
                waits=[(sem["vX1"], vxc)], incs=[(sem["pX1"], 1)])

    n_pair = 0
    n_tile = 0
    n_evac = 0

    for t in range(NT):
        nkt = 4 * (t + 1)
        q0 = QT * t
        for p in range(2):
            base = n_tile

            def emit_qk(j, tile_idx, first2=False, q0=q0, p=p):
                sps = s_ps[tile_idx % 2]
                waits = []
                if tile_idx >= 2:
                    waits.append((sem["xp"], tile_idx - 1))
                if first2:
                    waits.append((sem["rc"], P.count(sem["rc"])))
                for hh in (0, 1):
                    h = 2 * p + hh
                    P.emit("tensor", lambda e, sps=sps, j=j, hh=hh, h=h, q0=q0: e.matmul(
                        sps[:, hh * QT:(hh + 1) * QT],
                        k_all[:, j * 128:(j + 1) * 128],
                        q_all[:, h * T + q0: h * T + q0 + QT],
                        start=True, stop=True, skip_group_check=True),
                        waits=waits if hh == 0 else [],
                        incs=[(sem["sp"], 1)] if hh == 1 else [])

            def emit_exp(tile_idx):
                sps = s_ps[tile_idx % 2]
                pb = p_sb[tile_idx % 2]
                waits = [(sem["sp"], tile_idx + 1)]
                if tile_idx >= 2:
                    waits.append((sem["pv"], tile_idx - 1))
                    waits.append((sem["za"], tile_idx - 1))
                P.emit("scalar", lambda e, sps=sps, pb=pb: e.activation(
                    pb[:], sps[:], AF.Exp, bias=c_exp, scale=EXP_SCALE),
                    waits=waits, incs=[(sem["xp"], 1)])

            def emit_mask(j, tile_idx, t=t):
                o = j - 4 * t
                pb = p_sb[tile_idx % 2]
                m = masks_sb[:, o * QT:(o + 1) * QT]
                mrep = dataclasses.replace(
                    m, ap=mybir.VecI64Pair([list(m.ap[0]), [0, 2], list(m.ap[1])]))
                pb3 = pb[:].rearrange("a (r n) -> a r n", r=2)
                P.emit("vector", lambda e, pb3=pb3, mrep=mrep: e.tensor_mul(pb3, pb3, mrep),
                       waits=[(sem["xp"], tile_idx + 1)], incs=[(sem["mk"], 1)])

            def emit_pvz(j, tile_idx, is_first, is_last, masked, n_pair=n_pair):
                pb = p_sb[tile_idx % 2]
                waits = ([(sem["mk"], P.count(sem["mk"]))] if masked
                         else [(sem["xp"], tile_idx + 1)])
                if is_first and n_pair >= 2:
                    waits = waits + [(sem["ev"], 2 * n_pair - 2)]
                for hh in (0, 1):
                    ob = o_ps[(n_pair % 2) * 2 + hh]
                    P.emit("tensor", lambda e, pb=pb, j=j, hh=hh, ob=ob, a=is_first, z=is_last: e.matmul(
                        ob[:], v_all[:, j * 128:(j + 1) * 128], pb[:, hh * QT:(hh + 1) * QT],
                        start=a, stop=z, skip_group_check=True),
                        waits=waits if hh == 0 else [],
                        incs=[(sem["pv"], 1)] if hh == 1 else [])

            def emit_zadd(j, tile_idx, is_first, masked, n_pair=n_pair):
                pb = p_sb[tile_idx % 2]
                zb = zacc_sb[n_pair % 2]
                waits = ([(sem["mk"], P.count(sem["mk"]))] if masked
                         else [(sem["xp"], tile_idx + 1)])
                if is_first and n_pair >= 2:
                    waits = waits + [(sem["zm"], 2 * (n_pair - 1))]
                if is_first:
                    P.emit("vector", lambda e, zb=zb, pb=pb: e.tensor_copy(zb[:], pb[:]),
                           waits=waits, incs=[(sem["za"], 1)])
                else:
                    P.emit("vector", lambda e, zb=zb, pb=pb: e.tensor_add(zb[:], zb[:], pb[:]),
                           waits=waits, incs=[(sem["za"], 1)])

            emit_qk(0, base, first2=True)
            if nkt > 1:
                emit_qk(1, base + 1, first2=True)
            for j in range(nkt):
                ti = base + j
                masked = j >= nkt - 4
                emit_exp(ti)
                if masked:
                    emit_mask(j, ti)
                emit_pvz(j, ti, j == 0, j == nkt - 1, masked)
                emit_zadd(j, ti, j == 0, masked)
                if j + 2 < nkt:
                    emit_qk(j + 2, ti + 2)
            n_tile += nkt

            zac = P.count(sem["za"])
            zmb = zacc_sb[n_pair % 2]
            for hh in (0, 1):
                zw0 = [(sem["za"], zac), (sem["rc"], 2 * n_pair)] if hh == 0 else []
                P.emit("tensor", lambda e, hh=hh, zmb=zmb: e.matmul(
                    z_ps[hh], c_ones32, zmb[:, hh * QT:(hh + 1) * QT],
                    start=True, stop=True, skip_group_check=True),
                    waits=zw0, incs=[(sem["zm"], 1)])
            zmc = P.count(sem["zm"])
            for hh in (0, 1):
                h = 2 * p + hh
                zw = [(sem["zm"], zmc)]
                if n_evac >= 2:
                    zw.append((sem["dZ"], 32 * (n_evac - 2) + 16))
                P.emit("vector", lambda e, hh=hh, b=n_evac % 2: e.reciprocal(zinv_sb[b][:], z_ps[hh]),
                       waits=zw, incs=[(sem["rc"], 1)])
                rcc = P.count(sem["rc"])
                row = t * HPC + h
                P.emit("sync", lambda e, b=n_evac % 2, row=row: e.dma_start(
                    out=zscr[row:row + 1, :], in_=zinv_sb[b][:]),
                    waits=[(sem["rc"], rcc)], incs=[(sem["dZ"], 16)])
                dz1 = P.count(sem["dZ"])
                bw = [(sem["dZ"], dz1)]
                if n_evac >= 2:
                    bw.append((sem["ev"], n_evac - 1))
                P.emit("sync", lambda e, b=n_evac % 2, row=row: e.dma_start(
                    out=zb_sb[b][:], in_=zscr[row:row + 1, :].partition_broadcast(128)),
                    waits=bw, incs=[(sem["dZ"], 16)])
                dz2 = P.count(sem["dZ"])
                ww = [(sem["dZ"], dz2)]
                if n_evac >= 2:
                    ww.append((sem["dCT"], 16 * (n_evac - 1)))
                ob = o_ps[(n_pair % 2) * 2 + hh]
                P.emit("vector", lambda e, hh=hh, ob=ob, b=n_evac % 2: e.tensor_mul(
                    oo_sb[b][:], ob[:], zb_sb[b][:]),
                    waits=ww, incs=[(sem["ev"], 1)])
                evc = P.count(sem["ev"])
                qq, col = t // 4, (t % 4) * QT
                P.emit("sync", lambda e, h=h, qq=qq, col=col, b=n_evac % 2: e.dma_start(
                    out=contrib[qq][h * 128:(h + 1) * 128, col:col + QT], in_=oo_sb[b][:]),
                    waits=[(sem["ev"], evc)], incs=[(sem["dCT"], 16)])
                n_evac += 1
            n_pair += 1

        if t % 4 == 3:
            qq = t // 4
            ctc = P.count(sem["dCT"])
            P.emit("gpsimd", lambda e, qq=qq: e.collective_compute(
                "AllGather", OP.bypass, [[0, 1, 2, 3], [4, 5, 6, 7]],
                [contrib[qq].opt()], [gathered[qq * C:(qq + 1) * C, :].opt()]),
                waits=[(sem["dCT"], ctc)], incs=[(sem["cc"], 1)])
            P.emit("gpsimd", lambda e: e.tensor_copy(ccd_sb[:], ones_sb[:]),
                   incs=[(sem["ccd"], 1)])

    pp2.close()
    stacks.remove(pp2)
    pp3 = push()
    wo_ps = pp3.enter_context(nc.psum_tensor("wo_ps", [128, TB], F32))
    st_ps = pp3.enter_context(nc.psum_tensor("st_ps", [1, TB], F32))
    for _qq in range(4):
        emit_post(_qq)

    if _DEBUG and _STAGE == 2:
        P.emit("sync", lambda e: e.dma_start(out=dbg["gat"], in_=gathered),
               waits=[(sem["cc"], 4)], incs=[(sem["dbgs"], 16)])
    if _STAGE == 2:
        P.wait("sync", sem["dbgs"], P.count(sem["dbgs"]))
        P.wait("gpsimd", sem["cc"], 4)
        return finalize(nc, P, stacks)

    x1_ready = P.count(sem["pX1"])
    nx1w = P.count(sem["dX1w"])

    pp3.close()
    stacks.remove(pp3)
    ppffa = push()
    g_ps = [ppffa.enter_context(nc.psum_tensor(f"g_ps{m}", [128, TB], F32)) for m in range(4)]
    u_ps = [ppffa.enter_context(nc.psum_tensor(f"u_ps{m}", [128, TB], F32)) for m in range(4)]

    sA2.close()
    stacks.remove(sA2)
    sW.close()
    stacks.remove(sW)
    sC.close()
    stacks.remove(sC)
    sG = push()
    NSL = 11
    wgu_sb = [sG.enter_context(nc.sbuf_tensor(f"wgu{i}", [128, NCH * 128], BF16)) for i in range(NSL)]
    sg_sb = sG.enter_context(nc.sbuf_tensor("sg_sb", [128, 4 * TB], F32))
    hf_sb = sG.enter_context(nc.sbuf_tensor("hf_sb", [128, 4 * TB], F32))
    mma_sb = sG.enter_context(nc.sbuf_tensor("mma_sb", [128, 3 * TB], F32))
    t2_sb = sG.enter_context(nc.sbuf_tensor("t2_sb", [128, TB], F32))
    msk_sb = sG.enter_context(nc.sbuf_tensor("msk_sb", [128, 4 * TB], F32))
    hsp_sb = [sG.enter_context(nc.sbuf_tensor(f"hspt{i}", [128, TB], BF16)) for i in range(2)]
    wdc_sb = [sG.enter_context(nc.sbuf_tensor(f"wdc{i}", [128, C], BF16)) for i in range(2)]

    def wd_cast(r):
        slot = wdc_sb[r % 2]
        cw = [(sem["pX1"], x1_ready)] if r == 0 else []
        if r >= 2:
            cw.append((sem["dWB"], 32 * (r - 1)))
        P.emit("gpsimd", lambda e, slot=slot, r=r: e.dma_start(
            out=slot[:], in_=wd_h[r * 128:(r + 1) * 128, :]),
            waits=cw, incs=[(sem["dWC"], 16)])
        P.emit("gpsimd", lambda e, slot=slot, r=r: e.dma_start(
            out=wd_bf[r * 128:(r + 1) * 128, :], in_=slot[:, 0:1024]),
            waits=[(sem["dWC"], 16 * (r + 1))], incs=[(sem["dWB"], 16)])
        P.emit("gpsimd", lambda e, slot=slot, r=r: e.dma_start(
            out=wd_bf[I + r * 128:I + (r + 1) * 128, :], in_=slot[:, 1024:2048]),
            incs=[(sem["dWB"], 16)])

    def gu_thresh(cid):
        G, rem = divmod(cid, 8)
        return 8 * (NTB * G + (NTB - 1)) + rem + 1

    def wgu_load(cid):
        G, rem = divmod(cid, 8)
        w, m = divmod(rem, 4)
        src = wg_h if w == 0 else wu_h
        ch = G * 4 + m
        slot = wgu_sb[cid % NSL]
        waits = [(sem["pX1"], x1_ready)] if cid == 0 else []
        if cid >= NSL:
            waits.append((sem["pGU"], gu_thresh(cid - NSL)))
        P.emit("gpsimd", lambda e, slot=slot, src=src, ch=ch: e.dma_start(
            out=slot[:], in_=src[ch * 128:(ch + 1) * 128, :]),
            waits=waits, incs=[(sem["dGU"], 16)])

    n_loaded = 0
    for _ in range(NSL):
        wgu_load(n_loaded)
        n_loaded += 1

    n_spt = 0
    A = mma_sb[:, 0:TB]
    Bm = mma_sb[:, TB:2 * TB]
    Cm = mma_sb[:, 2 * TB:3 * TB]
    for G in range(16):
        for hb in range(NTB):
            it = G * NTB + hb
            tok0 = hb * TB
            for w in range(2):
                for m in range(4):
                    cid = G * 8 + w * 4 + m
                    slot = wgu_sb[cid % NSL]
                    ps = (g_ps if w == 0 else u_ps)[m]
                    waits = [(sem["dGU"], 16 * (cid + 1))]
                    if it >= 1:
                        if w == 0:
                            waits.append((sem["aSI"], 4 * (it - 1) + m + 1))
                        else:
                            waits.append((sem["vH"], 4 * (it - 1) + m + 1))
                    elif w == 0 and m == 0:
                        waits.append((sem["pX1"], x1_ready))
                    for c in range(NCH):
                        P.emit("tensor", lambda e, ps=ps, slot=slot, c=c, hb=hb: e.matmul(
                            ps[:], slot[:, c * 128:(c + 1) * 128],
                            x1q(c, hb),
                            start=(c == 0), stop=(c == NCH - 1), skip_group_check=True),
                            waits=waits if c == 0 else [],
                            incs=[(sem["pGU"], 1)] if c == NCH - 1 else [])
                    if hb == NTB - 1 and n_loaded < 128:
                        wgu_load(n_loaded)
                        n_loaded += 1
            pg = P.count(sem["pGU"])
            for m in range(4):
                sw = [(sem["pGU"], pg - 4)] if m == 0 else []
                if it >= 1:
                    sw.append((sem["vAB"], 4 * (it - 1) + m + 1))
                P.emit("scalar", lambda e, m=m: e.activation(
                    sg_sb[:, m * TB:(m + 1) * TB], g_ps[m][:], AF.Silu),
                    waits=sw, incs=[(sem["aSI"], 1)])
            asic = P.count(sem["aSI"])
            for m in range(4):
                P.emit("vector", lambda e, m=m: e.tensor_mul(
                    hf_sb[:, m * TB:(m + 1) * TB], u_ps[m][:], sg_sb[:, m * TB:(m + 1) * TB]),
                    waits=[(sem["pGU"], pg), (sem["aSI"], asic)] if m == 0 else [],
                    incs=[(sem["vH"], 1)])
            a = [sg_sb[:, m * TB:(m + 1) * TB] for m in range(4)]
            for m in range(4):
                P.emit("vector", lambda e, m=m: e.scalar_tensor_tensor(
                    a[m], hf_sb[:, m * TB:(m + 1) * TB], c_neg1,
                    hf_sb[:, m * TB:(m + 1) * TB], OP.mult, OP.max))
            P.emit("vector", lambda e: e.tensor_max(A, a[0], a[1]))
            P.emit("vector", lambda e: e.tensor_max(Bm, a[2], a[3]))
            P.emit("vector", lambda e: e.tensor_tensor(Cm, A, Bm, OP.min))
            P.emit("vector", lambda e: e.tensor_tensor(A, a[0], a[1], OP.min))
            P.emit("vector", lambda e: e.tensor_tensor(Bm, a[2], a[3], OP.min))
            P.emit("vector", lambda e: e.tensor_max(A, A, Bm))
            P.emit("vector", lambda e: e.tensor_max(t2_sb[:], A, Cm))
            for m in range(4):
                P.emit("vector", lambda e, m=m: e.tensor_tensor(
                    msk_sb[:, m * TB:(m + 1) * TB], a[m], t2_sb[:], OP.is_ge),
                    incs=[(sem["vAB"], 1)])
            for m in range(4):
                buf = hsp_sb[n_spt % 2]
                ww = [(sem["dHW"], 16 * (n_spt - 1))] if n_spt >= 2 else []
                P.emit("vector", lambda e, m=m, buf=buf: e.tensor_mul(
                    buf[:], hf_sb[:, m * TB:(m + 1) * TB], msk_sb[:, m * TB:(m + 1) * TB]),
                    waits=ww, incs=[(sem["vSP"], 1)])
                vspc = P.count(sem["vSP"])
                row0 = hb * I + (G * 4 + m) * 128
                P.emit("sync", lambda e, buf=buf, row0=row0: e.dma_start(
                    out=hsp_dram[row0:row0 + 128, :], in_=buf[:]),
                    waits=[(sem["vSP"], vspc)], incs=[(sem["dHW"], 16)])
                n_spt += 1
        if hb == NTB - 1:
            for rr in range(4 * G, 4 * (G + 1)):
                wd_cast(rr)
    hsp_writes = P.count(sem["dHW"])

    if _DEBUG and _STAGE == 35:
        P.wait("sync", sem["dHW"], hsp_writes)
        P.emit("sync", lambda e: e.dma_start(out=dbg["hsp"], in_=hsp_dram),
               incs=[(sem["dbgs"], 16)])
    if _STAGE == 35:
        P.wait("sync", sem["dbgs"], P.count(sem["dbgs"]))
        P.wait("sync", sem["dHW"], hsp_writes)
        return finalize(nc, P, stacks)

    ppffa.close()
    stacks.remove(ppffa)
    ppffb = push()
    d_ps = [ppffb.enter_context(nc.psum_tensor(f"d_ps{i}", [128, TB], F32)) for i in range(8)]

    sG.close()
    stacks.remove(sG)
    sA.close()
    stacks.remove(sA)
    sH = push()
    NWS = 6
    wd_sb = [sH.enter_context(nc.sbuf_tensor(f"wd{i}", [128, 1024], BF16)) for i in range(NWS)]
    hst_sb = [sH.enter_context(nc.sbuf_tensor(f"hst{i}", [128, TB], BF16)) for i in range(NWS)]
    x1r_sb = [sH.enter_context(nc.sbuf_tensor(f"x1r{i}", [128, TB], F32)) for i in range(8)]
    r2nb_sb = [sH.enter_context(nc.sbuf_tensor(f"r2nb{i}", [128, NCH * TB], F32)) for i in range(2)]
    tmpb_sb = [sH.enter_context(nc.sbuf_tensor(f"tmpb{i}", [128, TB], BF16)) for i in range(2)]

    g0 = [(sem["vSP"], 256), (sem["pGU"], 8 * 64), (sem["dWB"], 32 * 64),
          (sem["dHW"], hsp_writes)]
    n_wd = 0
    n_ev2 = 0
    nfin = 0
    for nb in range(NTB):
        r2b = r2nb_sb[nb % 2]
        for dh in range(2):
            grp = 2 * nb + dh
            for k in range(8):
                d = dh * 8 + k
                x1r = x1r_sb[k]
                xw = (list(g0) if grp == 0 else [(sem["vR2"], 8 * (grp - 1) + k + 1)])
                P.emit("sync", lambda e, x1r=x1r, d=d, nb=nb: e.dma_start(
                    out=x1r[:], in_=x1_dram[d * 128:(d + 1) * 128, nb * TB:(nb + 1) * TB]),
                    waits=xw + [(sem["dX1w"], nx1w)], incs=[(sem["dX1r"], 16)])
            x1_ready = P.count(sem["dX1r"])
            for r in range(64):
                slot = wd_sb[n_wd % NWS]
                hslot = hst_sb[n_wd % NWS]
                ww = list(g0) if n_wd < NWS else [(sem["pWD"], n_wd - (NWS - 1))]
                q = "gpsimd" if (n_wd % 2 == 0) else "sync"
                wdsem = sem["dWD"] if (n_wd % 2 == 0) else sem["dWD2"]
                P.emit(q, lambda e, slot=slot, r=r, dh=dh: e.dma_start(
                    out=slot[:], in_=wd_bf[dh * I + r * 128: dh * I + (r + 1) * 128, :]),
                    waits=ww, incs=[(wdsem, 16)])
                hww = list(g0) if n_wd < NWS else [(sem["pWD"], n_wd - (NWS - 1))]
                P.emit("sync", lambda e, hslot=hslot, r=r, nb=nb: e.dma_start(
                    out=hslot[:], in_=hsp_dram[nb * I + r * 128: nb * I + (r + 1) * 128, :]),
                    waits=hww, incs=[(sem["dHR"], 16)])
                wd_ready = P.count(wdsem)
                hr_ready = P.count(sem["dHR"])
                for dc in range(8):
                    waits = [(wdsem, wd_ready), (sem["dHR"], hr_ready)]
                    if r == 0 and dc == 0 and n_ev2 > 0:
                        waits.append((sem["vR2"], n_ev2))
                    if r == 0 and dc == 0 and dh == 0 and nb >= 1:
                        waits.append((sem["vLN2"], 3 * nb - 2))
                    P.emit("tensor", lambda e, dc=dc, slot=slot, hslot=hslot, r=r: e.matmul(
                        d_ps[dc][:], slot[:, dc * 128:(dc + 1) * 128], hslot[:],
                        start=(r == 0), stop=(r == 63), skip_group_check=True),
                        waits=waits if dc == 0 else [],
                        incs=[(sem["pWD"], 1)] if dc == 7 else [])
                n_wd += 1
            for dc in range(8):
                d = dh * 8 + dc
                x1r = x1r_sb[dc]
                P.emit("vector", lambda e, r2b=r2b, d=d, dc=dc, x1r=x1r: e.tensor_add(
                    r2b[:, d * TB:(d + 1) * TB], d_ps[dc][:], x1r[:]),
                    waits=[(sem["pWD"], n_wd), (sem["dX1r"], x1_ready)],
                    incs=[(sem["vR2"], 1)])
                n_ev2 += 1

        ev_all = P.count(sem["vR2"])
        for c2 in range(NCH):
            sl = r2b[:, c2 * TB:(c2 + 1) * TB]
            tmp = tmpb_sb[c2 % 2]
            sqt = sq_sb[c2 % 2]
            cw = [(sem["vR2"], ev_all)] if c2 == 0 else (
                [(sem["pST2"], 16 * nb + c2 - 1)] if c2 >= 2 else [])
            P.emit("vector", lambda e, tmp=tmp, sl=sl: e.tensor_copy(tmp[:], sl),
                   waits=cw, incs=[(sem["pSQ2"], 1)])
            P.emit("tensor", lambda e, tmp=tmp, c2=c2: e.matmul(
                d_ps[6][0:1, 0:TB], ones_sb[:, 0:1], tmp[:],
                start=(c2 == 0), stop=(c2 == NCH - 1), skip_group_check=True),
                waits=[(sem["pSQ2"], P.count(sem["pSQ2"]))], incs=[(sem["pST2"], 1)])
            sqw = [(sem["zS2"], 16 * nb + c2 - 1)] if c2 >= 2 else []
            P.emit("vector", lambda e, sqt=sqt, tmp=tmp: e.tensor_mul(sqt[:], tmp[:], tmp[:]),
                   waits=sqw, incs=[(sem["aSQ"], 1)])
            P.emit("tensor", lambda e, sqt=sqt, c2=c2: e.matmul(
                d_ps[7][0:1, 0:TB], ones_sb[:, 0:1], sqt[:],
                start=(c2 == 0), stop=(c2 == NCH - 1), skip_group_check=True),
                waits=[(sem["aSQ"], P.count(sem["aSQ"]))], incs=[(sem["zS2"], 1)])

        l_mu = lnt_sb[0:1, 0:TB]
        l_m2 = lnt_sb[0:1, TB:2 * TB]
        l_t = lnt_sb[0:1, 2 * TB:3 * TB]
        l_r = lnt_sb[0:1, 3 * TB:4 * TB]
        stw = [(sem["pST2"], 16 * (nb + 1)), (sem["zS2"], 16 * (nb + 1))]
        if nb >= 1:
            stw.append((sem["dLN2"], 64 * (nb - 1) + 32))
        P.emit("vector", lambda e: e.tensor_scalar(l_mu, d_ps[6][0:1, 0:TB], c_i2048, None, OP.mult),
               waits=stw)
        P.emit("vector", lambda e: e.tensor_scalar(l_m2, d_ps[7][0:1, 0:TB], c_i2048, None, OP.mult))
        P.emit("vector", lambda e: e.tensor_mul(l_t, l_mu, l_mu))
        P.emit("vector", lambda e: e.tensor_sub(l_r, l_m2, l_t), incs=[(sem["vLN2"], 1)])
        vlc = P.count(sem["vLN2"])
        P.emit("scalar", lambda e: e.activation(l_t, l_r, AF.Sqrt, bias=c_eps),
               waits=[(sem["vLN2"], vlc)], incs=[(sem["vLN2"], 1)])
        vlc = P.count(sem["vLN2"])
        P.emit("vector", lambda e: e.reciprocal(l_r, l_t), waits=[(sem["vLN2"], vlc)])
        P.emit("vector", lambda e: e.tensor_mul(l_m2, l_mu, l_r), incs=[(sem["vLN2"], 1)])
        vlc = P.count(sem["vLN2"])
        P.emit("sync", lambda e, nb=nb: e.dma_start(
            out=ln_dram[0:1, nb * TB:(nb + 1) * TB], in_=l_r),
            waits=[(sem["vLN2"], vlc)], incs=[(sem["dLN2"], 16)])
        P.emit("sync", lambda e, nb=nb: e.dma_start(
            out=ln_dram[1:2, nb * TB:(nb + 1) * TB], in_=l_m2),
            incs=[(sem["dLN2"], 16)])
        dlc = P.count(sem["dLN2"])
        P.emit("sync", lambda e, nb=nb: e.dma_start(
            out=rstd_b[:, nb * TB:(nb + 1) * TB],
            in_=ln_dram[0:1, nb * TB:(nb + 1) * TB].partition_broadcast(128)),
            waits=[(sem["dLN2"], dlc)], incs=[(sem["dLN2"], 16)])
        P.emit("sync", lambda e, nb=nb: e.dma_start(
            out=nu_b[:, nb * TB:(nb + 1) * TB],
            in_=ln_dram[1:2, nb * TB:(nb + 1) * TB].partition_broadcast(128)),
            incs=[(sem["dLN2"], 16)])
        dl_all = P.count(sem["dLN2"])

        for c2 in range(NCH):
            xm = x1m_sb[nfin % 2]
            xv = x1t_sb[nfin % 2]
            fw = [(sem["dLN2"], dl_all)] if c2 == 0 else []
            P.emit("vector", lambda e, xm=xm, r2b=r2b, c2=c2, nb=nb: e.tensor_mul(
                xm[:], r2b[:, c2 * TB:(c2 + 1) * TB], rstd_b[:, nb * TB:(nb + 1) * TB]),
                waits=fw)
            ow = [(sem["dOUT"], 16 * (nfin - 1))] if nfin >= 2 else []
            P.emit("vector", lambda e, xv=xv, xm=xm, nb=nb: e.tensor_sub(
                xv[:], xm[:], nu_b[:, nb * TB:(nb + 1) * TB]),
                waits=ow, incs=[(sem["vF"], 1)])
            vfc = P.count(sem["vF"])
            P.emit("sync", lambda e, xv=xv, c2=c2, nb=nb: e.dma_start(
                out=out_fm[c2 * 128:(c2 + 1) * 128, nb * TB:(nb + 1) * TB], in_=xv[:]),
                waits=[(sem["vF"], vfc)], incs=[(sem["dOUT"], 16)])
            nfin += 1

    P.wait("sync", sem["dOUT"], 16 * nfin)
    return finalize(nc, P, stacks)


def finalize(nc, P, stacks):
    with nc.Block() as block:
        @block.tensor
        def _(e):
            P.replay("tensor", e)

        @block.vector
        def _(e):
            P.replay("vector", e)

        @block.scalar
        def _(e):
            P.replay("scalar", e)

        @block.gpsimd
        def _(e):
            P.replay("gpsimd", e)

        @block.sync
        def _(e):
            P.replay("sync", e)
    for s in reversed(stacks):
        try:
            s.close()
        except Exception:
            pass
    return nc


def _host_prep(x, Wq, Wk, Wv, Wo, wgm, wum, wdm):
    x = np.asarray(x, dtype=np.float32)
    Wq = np.asarray(Wq, np.float32); Wk = np.asarray(Wk, np.float32)
    Wv = np.asarray(Wv, np.float32); Wo = np.asarray(Wo, np.float32)
    wgm = np.asarray(wgm, np.float32); wum = np.asarray(wum, np.float32)
    wdm = np.asarray(wdm, np.float32)

    inv = (1.0 / 10000.0 ** (np.arange(0, D, 2, dtype=np.float32) / np.float32(D))).astype(np.float32)
    f = np.arange(T, dtype=np.float32)[:, None] * inv[None, :]
    cos_t = np.ascontiguousarray(np.concatenate([np.cos(f)] * 2, axis=1).T)
    sin_t = np.ascontiguousarray(np.concatenate([np.sin(f)] * 2, axis=1).T)

    masks = np.zeros((128, 4 * QT), dtype=ml_dtypes.bfloat16)
    for o in range(4):
        kk = 128 * o + np.arange(128)[:, None]
        qq = np.arange(QT)[None, :]
        masks[:, o * QT:(o + 1) * QT] = (kk <= qq).astype(ml_dtypes.bfloat16)

    ones_bf = np.ones((128, 1), dtype=ml_dtypes.bfloat16)
    cvec = np.zeros((128, 8), dtype=np.float32)
    cvec[:, 0] = EXP_SHIFT
    cvec[:, 1] = 1e-5
    cvec[:, 2] = 1.0 / 2048.0
    cvec[:, 3] = -1.0
    cvec[:, 4] = 1.0

    Gi, mi, pi = np.meshgrid(np.arange(16), np.arange(4), np.arange(128), indexing="ij")
    idx = (4 * (128 * Gi + pi) + mi).reshape(-1)
    wgp = wgm[:, idx]
    wup = wum[:, idx]
    wd_p = np.ascontiguousarray(wdm[idx, :])
    wg_h = np.ascontiguousarray(
        wgp.reshape(NCH, 128, 64, 128).transpose(2, 1, 0, 3).reshape(64 * 128, NCH * 128))
    wu_h = np.ascontiguousarray(
        wup.reshape(NCH, 128, 64, 128).transpose(2, 1, 0, 3).reshape(64 * 128, NCH * 128))
    wo_h = np.ascontiguousarray(
        Wo.reshape(NCH, 128, NCH, 128).transpose(1, 2, 0, 3).reshape(128, NCH * C))

    xtl = [np.ascontiguousarray(
        x[b].reshape(NT, QT, NCH, 128).transpose(0, 3, 2, 1).reshape(NT * 128, NCH * QT))
        for b in range(B)]

    in_maps = []
    for c in range(8):
        b, g = c // 4, c % 4
        wq_c = Wq[:, 512 * g:512 * (g + 1)]
        wq_h = np.ascontiguousarray(
            wq_c.reshape(NCH, 128, 512).transpose(1, 0, 2).reshape(128, NCH * 512))
        wk_c = Wk[:, 128 * g:128 * (g + 1)]
        wk_hh = np.ascontiguousarray(
            wk_c.reshape(NCH, 128, 128).transpose(1, 0, 2).reshape(128, NCH * 128))
        wv_c = Wv[:, 128 * g:128 * (g + 1)]
        wv_hh = np.ascontiguousarray(
            wv_c.reshape(NCH, 128, 128).transpose(1, 0, 2).reshape(128, NCH * 128))
        toks = np.concatenate([np.arange(qq * TPQ + g * TB, qq * TPQ + (g + 1) * TB)
                               for qq in range(4)])
        in_maps.append({
            "x_tiles": xtl[b],
            "x_res": np.ascontiguousarray(x[b, toks, :].T),
            "wq_h": wq_h, "wk_h": wk_hh, "wv_h": wv_hh,
            "wo_h": wo_h, "wg_h": wg_h, "wu_h": wu_h, "wd_h": wd_p,
            "cos_t": cos_t, "sin_t": sin_t, "masks": masks,
            "ones_bf": ones_bf, "cvec": cvec,
        })
    return in_maps


def kernel(L0_attn_x, L0_attn_Wq, L0_attn_Wk, L0_attn_Wv, L0_attn_Wo,
           L0_ffn_w_gate, L0_ffn_w_up, L0_ffn_w_down):
    in_maps = _host_prep(L0_attn_x, L0_attn_Wq, L0_attn_Wk, L0_attn_Wv, L0_attn_Wo,
                         L0_ffn_w_gate, L0_ffn_w_up, L0_ffn_w_down)
    nc = build_program()
    res = run_bass_kernel_spmd(nc, in_maps, core_ids=list(range(8)))
    out = np.empty((B, T, C), dtype=np.float32)
    for c in range(8):
        b, g = c // 4, c % 4
        o = np.asarray(res.results[c]["out_fm"]).T
        for qq in range(4):
            out[b, qq * TPQ + g * TB: qq * TPQ + (g + 1) * TB, :] = o[qq * TB:(qq + 1) * TB, :]
    return out


# version 21 (final)
# speedup vs baseline: 1.0157x; 1.0035x over previous
import dataclasses
import contextlib
import numpy as np
import ml_dtypes
import concourse.bass as bass
import concourse.mybir as mybir
from concourse.bass_utils import run_bass_kernel_spmd

F32 = mybir.dt.float32
BF16 = mybir.dt.bfloat16
AF = mybir.ActivationFunctionType
OP = mybir.AluOpType

B, T, C = 2, 8192, 2048
HPC = 4
D = 128
I = 8192
QT = 512
NT = T // QT
TPQ = 2048
TB = 512
NTB = TPQ // TB
NCH = C // 128
EXP_SHIFT = -12.0
EXP_SCALE = float(1.0 / np.sqrt(np.float64(D)))

_STAGE = 4
_DEBUG = False


class Prog:
    ENGINES = ("tensor", "vector", "scalar", "gpsimd", "sync")

    def __init__(self):
        self.ops = {e: [] for e in self.ENGINES}
        self.cnt = {}

    def count(self, sem):
        return self.cnt.get(sem.name, 0)

    def emit(self, eng, build, waits=(), incs=()):
        incs = tuple(incs)
        for s, n in incs:
            self.cnt[s.name] = self.cnt.get(s.name, 0) + n
        self.ops[eng].append((build, tuple(waits), incs))

    def wait(self, eng, s, v):
        self.ops[eng].append((None, ((s, v),), ()))

    def replay(self, name, eng):
        for build, waits, incs in self.ops[name]:
            for s, v in waits:
                if v > 0:
                    eng.wait_ge(s, v)
            if build is None:
                continue
            ins = build(eng)
            for s, n in incs:
                ins = ins.then_inc(s, n)


def build_program():
    nc = bass.Bass("TRN2", target_bir_lowering=False, debug=False)

    x_tiles = nc.dram_tensor("x_tiles", [NT * 128, NCH * QT], F32, kind="ExternalInput").ap()
    x_res = nc.dram_tensor("x_res", [C, TPQ], F32, kind="ExternalInput").ap()
    wq_h = nc.dram_tensor("wq_h", [128, NCH * HPC * D], F32, kind="ExternalInput").ap()
    wk_h = nc.dram_tensor("wk_h", [128, NCH * D], F32, kind="ExternalInput").ap()
    wv_h = nc.dram_tensor("wv_h", [128, NCH * D], F32, kind="ExternalInput").ap()
    wo_h = nc.dram_tensor("wo_h", [128, NCH * C], F32, kind="ExternalInput").ap()
    wg_h = nc.dram_tensor("wg_h", [64 * 128, NCH * 128], F32, kind="ExternalInput").ap()
    wu_h = nc.dram_tensor("wu_h", [64 * 128, NCH * 128], F32, kind="ExternalInput").ap()
    wd_h = nc.dram_tensor("wd_h", [I, C], F32, kind="ExternalInput").ap()
    cos_in = nc.dram_tensor("cos_t", [D, T], F32, kind="ExternalInput").ap()
    sin_in = nc.dram_tensor("sin_t", [D, T], F32, kind="ExternalInput").ap()
    masks_in = nc.dram_tensor("masks", [128, 4 * QT], BF16, kind="ExternalInput").ap()
    ones_in = nc.dram_tensor("ones_bf", [128, 1], BF16, kind="ExternalInput").ap()
    cvec_in = nc.dram_tensor("cvec", [128, 8], F32, kind="ExternalInput").ap()

    out_fm = nc.dram_tensor("out_fm", [C, TPQ], F32, kind="ExternalOutput").ap()

    contrib = [nc.dram_tensor(f"contrib{q}", [HPC * D, TPQ], BF16, kind="Internal").ap()
               for q in range(4)]
    gathered = nc.dram_tensor("gathered", [4 * C, TPQ], BF16, kind="Internal").ap()
    zscr = nc.dram_tensor("zscr", [NT * HPC, QT], F32, kind="Internal").ap()
    r1_dram = nc.dram_tensor("r1_dram", [C, TPQ], F32, kind="Internal").ap()
    x1_dram = nc.dram_tensor("x1_dram", [C, TPQ], F32, kind="Internal").ap()
    hsp_dram = nc.dram_tensor("hsp_dram", [NTB * I, TB], BF16, kind="Internal").ap()
    wd_bf = nc.dram_tensor("wd_bf", [2 * I, 1024], BF16, kind="Internal").ap()
    ln_dram = nc.dram_tensor("ln_dram", [2, 2 * TPQ], F32, kind="Internal").ap()

    dbg = {}
    if _DEBUG:
        if _STAGE == 1:
            dbg["q"] = nc.dram_tensor("dbg_q", [128, HPC * T], BF16, kind="ExternalOutput").ap()
            dbg["k"] = nc.dram_tensor("dbg_k", [128, T], BF16, kind="ExternalOutput").ap()
            dbg["v"] = nc.dram_tensor("dbg_v", [128, T], BF16, kind="ExternalOutput").ap()
        if _STAGE == 2:
            dbg["gat"] = nc.dram_tensor("dbg_gat", [4 * C, TPQ], BF16, kind="ExternalOutput").ap()
        if _STAGE == 3:
            dbg["x1"] = nc.dram_tensor("dbg_x1", [C, TPQ], F32, kind="ExternalOutput").ap()
        if _STAGE == 35:
            dbg["hsp"] = nc.dram_tensor("dbg_hsp", [I, TPQ], BF16, kind="ExternalOutput").ap()
        if _STAGE == 37:
            dbg["r2"] = nc.dram_tensor("dbg_r2", [C, TPQ], F32, kind="ExternalOutput").ap()

    P = Prog()
    stacks = []

    def push():
        s = contextlib.ExitStack()
        stacks.append(s)
        return s

    st = push()
    sem = {}
    for name in ("dMC", "dW1", "dXT", "dCS", "pQ", "pK", "pV", "vQ", "vK", "vV",
                 "xtF", "csF", "za", "zm",
                 "sp", "xp", "mk", "pv", "rc", "ev", "dZ", "dCT", "cc",
                 "dWO", "dGT", "dXR", "pWO", "vR1", "dR1w", "pR1",
                 "pST1", "pSQ1", "vLN1", "dLN1", "dR1r", "vX1", "dX1w", "pX1",
                 "dGU", "pGU", "aSI", "vH", "vAB", "vSP", "dHW",
                 "dHR", "dWD", "pWD", "pFB", "vR2", "dWC", "dWB", "dR2w", "pR2", "dX1r",
                 "pST2", "pSQ2", "vLN2", "dLN2", "dR2r", "dOUT", "dbgs",
                 "aSQ", "zS2", "vF", "dWD2", "sS2", "ccd"):
        sem[name] = st.enter_context(nc.semaphore(name))

    ones_sb = st.enter_context(nc.sbuf_tensor("ones_sb", [128, 1], BF16))
    cvec_sb = st.enter_context(nc.sbuf_tensor("cvec_sb", [128, 8], F32))

    P.emit("sync", lambda e: e.dma_start(out=ones_sb[:], in_=ones_in), incs=[(sem["dMC"], 16)])
    P.emit("sync", lambda e: e.dma_start(out=cvec_sb[:], in_=cvec_in), incs=[(sem["dMC"], 16)])
    consts_ready = P.count(sem["dMC"])

    c_exp = cvec_sb[:, 0:1]
    c_eps = cvec_sb[0:1, 1:2]
    c_i2048 = cvec_sb[0:1, 2:3]
    c_neg1 = cvec_sb[:, 3:4]
    c_ones32 = cvec_sb[:, 4:5]

    sA = push()
    q_all = sA.enter_context(nc.sbuf_tensor("q_all", [128, HPC * T], BF16, side="right"))
    sA2 = push()
    k_all = sA2.enter_context(nc.sbuf_tensor("k_all", [128, T], BF16, side="right"))
    v_all = sA2.enter_context(nc.sbuf_tensor("v_all", [128, T], BF16, side="right"))

    sB = push()
    wq_sb = sB.enter_context(nc.sbuf_tensor("wq_sb", [128, NCH * HPC * D], BF16))
    wk_sb = sB.enter_context(nc.sbuf_tensor("wk_sb", [128, NCH * D], BF16))
    wv_sb = sB.enter_context(nc.sbuf_tensor("wv_sb", [128, NCH * D], BF16))
    xt = [sB.enter_context(nc.sbuf_tensor(f"xt{i}", [128, NCH * QT], BF16)) for i in range(2)]
    cs = [sB.enter_context(nc.sbuf_tensor(f"cs{i}", [128, 2 * QT], F32)) for i in range(2)]
    rtmp1 = sB.enter_context(nc.sbuf_tensor("rtmp1", [128, QT], F32))
    rtmp2 = sB.enter_context(nc.sbuf_tensor("rtmp2", [128, QT], F32))

    pp1 = push()
    q_ps = [pp1.enter_context(nc.psum_tensor(f"q_ps{i}", [128, QT], F32)) for i in range(2)]
    k_ps = [pp1.enter_context(nc.psum_tensor(f"k_ps{i}", [128, QT], F32)) for i in range(2)]
    v_ps = [pp1.enter_context(nc.psum_tensor(f"v_ps{i}", [128, 128], F32)) for i in range(2)]

    P.emit("gpsimd", lambda e: e.dma_start(out=wq_sb[:], in_=wq_h), incs=[(sem["dW1"], 16)])
    P.emit("gpsimd", lambda e: e.dma_start(out=wk_sb[:], in_=wk_h), incs=[(sem["dW1"], 16)])
    P.emit("gpsimd", lambda e: e.dma_start(out=wv_sb[:], in_=wv_h), incs=[(sem["dW1"], 16)])
    w1_ready = P.count(sem["dW1"])

    def rope_emit(ps, dst, tau, psem, pval, vsem):
        cst = cs[tau % 2]
        cs_ready = 32 * (tau + 1)
        P.emit("vector", lambda e: e.tensor_mul(rtmp1[:], ps[:], cst[:, 0:QT]),
               waits=[(sem[psem], pval), (sem["dCS"], cs_ready)])
        P.emit("vector", lambda e: e.tensor_mul(rtmp2[0:64, :], ps[64:128, :], cst[0:64, QT:2 * QT]))
        P.emit("vector", lambda e: e.tensor_mul(rtmp2[64:128, :], ps[0:64, :], cst[64:128, QT:2 * QT]))
        P.emit("vector", lambda e: e.tensor_sub(dst[0:64, :], rtmp1[0:64, :], rtmp2[0:64, :]))
        P.emit("vector", lambda e: e.tensor_add(dst[64:128, :], rtmp1[64:128, :], rtmp2[64:128, :]),
               incs=[(sem[vsem], 1)])

    for tau in range(NT):
        xb = xt[tau % 2]
        t0 = QT * tau
        xw = [(sem["pV"], 4 * (tau - 1))] if tau >= 2 else []
        P.emit("gpsimd", lambda e, xb=xb, tau=tau: e.dma_start(
            out=xb[:], in_=x_tiles[tau * 128:(tau + 1) * 128, :]),
            waits=xw, incs=[(sem["dXT"], 16)])
        xt_ready = P.count(sem["dXT"])
        cw = [(sem["vK"], tau - 1)] if tau >= 2 else []
        csb = cs[tau % 2]
        P.emit("sync", lambda e, csb=csb, t0=t0: e.dma_start(out=csb[:, 0:QT], in_=cos_in[:, t0:t0 + QT]),
               waits=cw, incs=[(sem["dCS"], 16)])
        P.emit("sync", lambda e, csb=csb, t0=t0: e.dma_start(out=csb[:, QT:2 * QT], in_=sin_in[:, t0:t0 + QT]),
               incs=[(sem["dCS"], 16)])

        for h in range(HPC):
            ps = q_ps[h % 2]
            idx = tau * HPC + h
            pw = [(sem["dW1"], w1_ready), (sem["dXT"], xt_ready)]
            if idx >= 2:
                pw.append((sem["vQ"], idx - 1))
            for c in range(NCH):
                P.emit("tensor", lambda e, ps=ps, c=c, h=h, xb=xb: e.matmul(
                    ps[:], wq_sb[:, c * HPC * D + h * D: c * HPC * D + (h + 1) * D],
                    xb[:, c * QT:(c + 1) * QT],
                    start=(c == 0), stop=(c == NCH - 1), skip_group_check=True),
                    waits=pw if c == 0 else [],
                    incs=[(sem["pQ"], 1)] if c == NCH - 1 else [])
            rope_emit(ps, q_all[:, h * T + t0: h * T + t0 + QT], tau, "pQ", P.count(sem["pQ"]), "vQ")

        ps = k_ps[tau % 2]
        pw = [(sem["dW1"], w1_ready), (sem["dXT"], xt_ready)]
        if tau >= 2:
            pw.append((sem["vK"], tau - 1))
        for c in range(NCH):
            P.emit("tensor", lambda e, ps=ps, c=c, xb=xb: e.matmul(
                ps[:], wk_sb[:, c * D:(c + 1) * D], xb[:, c * QT:(c + 1) * QT],
                start=(c == 0), stop=(c == NCH - 1), skip_group_check=True),
                waits=pw if c == 0 else [],
                incs=[(sem["pK"], 1)] if c == NCH - 1 else [])
        rope_emit(ps, k_all[:, t0:t0 + QT], tau, "pK", P.count(sem["pK"]), "vK")

        for s in range(4):
            ps = v_ps[s % 2]
            idx = tau * 4 + s
            pw = [(sem["dW1"], w1_ready), (sem["dXT"], xt_ready)]
            if idx >= 2:
                pw.append((sem["vV"], idx - 1))
            for c in range(NCH):
                P.emit("tensor", lambda e, ps=ps, c=c, s=s, xb=xb: e.matmul(
                    ps[:], xb[:, c * QT + s * 128: c * QT + (s + 1) * 128],
                    wv_sb[:, c * D:(c + 1) * D],
                    start=(c == 0), stop=(c == NCH - 1), skip_group_check=True),
                    waits=pw if c == 0 else [],
                    incs=[(sem["pV"], 1)] if c == NCH - 1 else [])
            P.emit("vector", lambda e, ps=ps, t0=t0, s=s: e.tensor_copy(
                v_all[:, t0 + s * 128: t0 + (s + 1) * 128], ps[:]),
                waits=[(sem["pV"], P.count(sem["pV"]))],
                incs=[(sem["vV"], 1)])

    qkv_done = (P.count(sem["vQ"]), P.count(sem["vK"]), P.count(sem["vV"]))

    if _DEBUG and _STAGE == 1:
        P.emit("sync", lambda e: e.dma_start(out=dbg["q"], in_=q_all[:]),
               waits=[(sem["vQ"], qkv_done[0])], incs=[(sem["dbgs"], 16)])
        P.emit("sync", lambda e: e.dma_start(out=dbg["k"], in_=k_all[:]),
               waits=[(sem["vK"], qkv_done[1])], incs=[(sem["dbgs"], 16)])
        P.emit("sync", lambda e: e.dma_start(out=dbg["v"], in_=v_all[:]),
               waits=[(sem["vV"], qkv_done[2])], incs=[(sem["dbgs"], 16)])
    if _STAGE == 1:
        P.wait("sync", sem["dbgs"], P.count(sem["dbgs"]))
        P.wait("vector", sem["vV"], qkv_done[2])
        return finalize(nc, P, stacks)

    pp1.close()
    stacks.remove(pp1)
    pp2 = push()
    s_ps = [pp2.enter_context(nc.psum_tensor(f"s_ps{i}", [128, 2 * QT], F32)) for i in range(2)]
    o_ps = [pp2.enter_context(nc.psum_tensor(f"o_ps{i}", [128, QT], F32)) for i in range(4)]
    z_ps = [s_ps[0][0:1, 0:QT], s_ps[0][0:1, QT:2 * QT]]

    sB.close()
    stacks.remove(sB)
    sF = push()
    sq_sb = [sF.enter_context(nc.sbuf_tensor(f"sq{i}", [128, TB], BF16)) for i in range(2)]
    lnt_sb = sF.enter_context(nc.sbuf_tensor("lnt_sb", [1, 4 * TB], F32))
    rstd_b = sF.enter_context(nc.sbuf_tensor("rstd_b", [128, TB], F32))
    nu_b = sF.enter_context(nc.sbuf_tensor("nu_b", [128, TB], F32))
    x1t_sb = [sF.enter_context(nc.sbuf_tensor(f"x1t{i}", [128, TB], F32)) for i in range(2)]
    x1m_sb = [sF.enter_context(nc.sbuf_tensor(f"x1m{i}", [128, TB], F32)) for i in range(2)]
    r1t_sb = [sF.enter_context(nc.sbuf_tensor(f"r1t{i}", [128, TB], F32)) for i in range(2)]

    sC = push()
    masks_sb = sC.enter_context(nc.sbuf_tensor("masks_sb", [128, 4 * QT], BF16))
    p_sb = [sC.enter_context(nc.sbuf_tensor(f"p_sb{i}", [128, 2 * QT], BF16)) for i in range(3)]
    zacc_sb = [sC.enter_context(nc.sbuf_tensor(f"zacc{i}", [128, 2 * QT], F32)) for i in range(2)]
    zinv_sb = [sC.enter_context(nc.sbuf_tensor(f"zinv{i}", [1, QT], F32)) for i in range(2)]
    zb_sb = [sC.enter_context(nc.sbuf_tensor(f"zb{i}", [128, QT], F32)) for i in range(2)]
    oo_sb = [sC.enter_context(nc.sbuf_tensor(f"oo{i}", [128, QT], BF16)) for i in range(2)]
    ccd_sb = sC.enter_context(nc.sbuf_tensor("ccd_sb", [128, 1], BF16))
    gat_sb = sC.enter_context(nc.sbuf_tensor("gat_sb", [128, NCH * TB], BF16))
    sqq_sb = sC.enter_context(nc.sbuf_tensor("sqq_sb", [128, NCH * TB], BF16))
    xr_sb = [sC.enter_context(nc.sbuf_tensor(f"xr{i}", [128, TB], F32)) for i in range(2)]
    tbq_sb = [sC.enter_context(nc.sbuf_tensor(f"tbq{i}", [128, TB], BF16)) for i in range(2)]

    P.emit("sync", lambda e: e.dma_start(out=masks_sb[:], in_=masks_in),
           waits=[(sem["vV"], qkv_done[2])], incs=[(sem["dMC"], 16)])
    consts_ready = P.count(sem["dMC"])

    sW = push()
    wo_sb = [sW.enter_context(nc.sbuf_tensor(f"wo_sb{i}", [128, NCH * 128], BF16))
             for i in range(2)]

    P.wait("tensor", sem["vQ"], qkv_done[0])
    P.wait("tensor", sem["vK"], qkv_done[1])
    P.wait("tensor", sem["vV"], qkv_done[2])
    P.wait("tensor", sem["dMC"], 32)
    P.wait("scalar", sem["dMC"], consts_ready)
    P.wait("vector", sem["dMC"], consts_ready)

    pid = nc.partition_id([mybir.EngineType.SP])
    grp_off = (pid % 4) * TB

    def x1q(c, hb):
        o = (c // 4) * T + hb * TPQ + (c % 4) * TB
        return q_all[:, o:o + TB]

    def emit_post(qq):
        gw = [(sem["cc"], 4), (sem["ccd"], 4)]
        if qq >= 1:
            gw.append((sem["pWO"], 256 * qq))
        src0 = gathered[0:128, 0:TB]
        src3 = dataclasses.replace(
            src0, offset=src0.offset + qq * (C * TPQ) + grp_off,
            ap=mybir.VecI64Pair([[TPQ, 128], [128 * TPQ, NCH], [1, TB]]))
        gb3 = gat_sb[:].rearrange("p (c n) -> p c n", c=NCH)
        P.emit("sync", lambda e, gb3=gb3, src3=src3: e.dma_start(out=gb3, in_=src3),
               waits=gw, incs=[(sem["dGT"], 16)])
        gt_ready = P.count(sem["dGT"])
        for c2 in range(NCH):
            ngrp = qq * NCH + c2
            wslot = wo_sb[ngrp % 2]
            loww = [(sem["pWO"], 16 * (ngrp - 1))] if ngrp >= 2 else []
            P.emit("gpsimd", lambda e, wslot=wslot, c2=c2: e.dma_start(
                out=wslot[:], in_=wo_h[:, c2 * C:(c2 + 1) * C]),
                waits=loww, incs=[(sem["dWO"], 16)])
            ps = wo_ps
            mw = [(sem["dWO"], P.count(sem["dWO"])), (sem["dGT"], gt_ready)]
            if ngrp >= 1:
                mw.append((sem["vR1"], ngrp))
            for c1 in range(NCH):
                P.emit("tensor", lambda e, c1=c1, wslot=wslot, ps=ps: e.matmul(
                    ps[:], wslot[:, c1 * 128:(c1 + 1) * 128],
                    gat_sb[:, c1 * TB:(c1 + 1) * TB],
                    start=(c1 == 0), stop=(c1 == NCH - 1), skip_group_check=True),
                    waits=mw if c1 == 0 else [],
                    incs=[(sem["pWO"], 1)])
            xrb = xr_sb[ngrp % 2]
            xw = [(sem["vR1"], ngrp - 1)] if ngrp >= 2 else []
            P.emit("gpsimd", lambda e, xrb=xrb, c2=c2, qq=qq: e.dma_start(
                out=xrb[:], in_=x_res[c2 * 128:(c2 + 1) * 128, qq * TB:(qq + 1) * TB]),
                waits=xw, incs=[(sem["dXR"], 16)])
            rt = r1t_sb[ngrp % 2]
            ww = [(sem["pWO"], 16 * (ngrp + 1)), (sem["dXR"], P.count(sem["dXR"]))]
            if ngrp >= 2:
                ww.append((sem["dR1w"], 16 * (ngrp - 1)))
                ww.append((sem["pR1"], ngrp - 1))
                ww.append((sem["vX1"], 16 * qq))
            P.emit("vector", lambda e, rt=rt, xrb=xrb, ps=ps: e.tensor_add(rt[:], ps[:], xrb[:]),
                   waits=ww, incs=[(sem["vR1"], 1)])
            vr = P.count(sem["vR1"])
            P.emit("sync", lambda e, rt=rt, c2=c2, qq=qq: e.dma_start(
                out=r1_dram[c2 * 128:(c2 + 1) * 128, qq * TB:(qq + 1) * TB], in_=rt[:]),
                waits=[(sem["vR1"], vr)], incs=[(sem["dR1w"], 16)])
            tb16 = tbq_sb[ngrp % 2]
            cw = [(sem["pST1"], ngrp - 1)] if ngrp >= 2 else []
            P.emit("vector", lambda e, tb16=tb16, rt=rt: e.tensor_copy(tb16[:], rt[:]),
                   waits=cw, incs=[(sem["pR1"], 1)])
            P.emit("vector", lambda e, tb16=tb16, c2=c2: e.tensor_mul(
                sqq_sb[:, c2 * TB:(c2 + 1) * TB], tb16[:], tb16[:]),
                incs=[(sem["pSQ1"], 1)])
            sw1 = [(sem["pR1"], P.count(sem["pR1"]))]
            if c2 == 0 and qq >= 1:
                sw1.append((sem["vLN1"], 5 * qq))
            P.emit("tensor", lambda e, tb16=tb16, c2=c2: e.matmul(
                st_ps[0:1, 0:TB], ones_sb[:, 0:1], tb16[:],
                start=(c2 == 0), stop=(c2 == NCH - 1), skip_group_check=True),
                waits=sw1, incs=[(sem["pST1"], 1)])
        l_mu = lnt_sb[0:1, 0:TB]
        l_m2 = lnt_sb[0:1, TB:2 * TB]
        l_t = lnt_sb[0:1, 2 * TB:3 * TB]
        l_r = lnt_sb[0:1, 3 * TB:4 * TB]
        stw = [(sem["pST1"], 16 * (qq + 1))]
        if qq >= 1:
            stw.append((sem["dLN1"], 64 * (qq - 1) + 32))
        P.emit("vector", lambda e: e.tensor_scalar(l_mu, st_ps[0:1, 0:TB], c_i2048, None, OP.mult),
               waits=stw, incs=[(sem["vLN1"], 1)])
        for c2 in range(NCH):
            s2w = ([(sem["vLN1"], 5 * qq + 1), (sem["pSQ1"], 16 * (qq + 1))]
                   if c2 == 0 else [])
            P.emit("tensor", lambda e, c2=c2: e.matmul(
                st_ps[0:1, 0:TB], ones_sb[:, 0:1], sqq_sb[:, c2 * TB:(c2 + 1) * TB],
                start=(c2 == 0), stop=(c2 == NCH - 1), skip_group_check=True),
                waits=s2w, incs=[(sem["sS2"], 1)])
        P.emit("vector", lambda e: e.tensor_scalar(l_m2, st_ps[0:1, 0:TB], c_i2048, None, OP.mult),
               waits=[(sem["sS2"], 16 * (qq + 1))], incs=[(sem["vLN1"], 1)])
        P.emit("vector", lambda e: e.tensor_mul(l_t, l_mu, l_mu))
        P.emit("vector", lambda e: e.tensor_sub(l_r, l_m2, l_t), incs=[(sem["vLN1"], 1)])
        P.emit("scalar", lambda e: e.activation(l_t, l_r, AF.Sqrt, bias=c_eps),
               waits=[(sem["vLN1"], 5 * qq + 3)], incs=[(sem["vLN1"], 1)])
        P.emit("vector", lambda e: e.reciprocal(l_r, l_t), waits=[(sem["vLN1"], 5 * qq + 4)])
        P.emit("vector", lambda e: e.tensor_mul(l_m2, l_mu, l_r), incs=[(sem["vLN1"], 1)])
        P.emit("sync", lambda e, qq=qq: e.dma_start(
            out=ln_dram[0:1, qq * TB:(qq + 1) * TB], in_=l_r),
            waits=[(sem["vLN1"], 5 * qq + 5)], incs=[(sem["dLN1"], 16)])
        P.emit("sync", lambda e, qq=qq: e.dma_start(
            out=ln_dram[1:2, qq * TB:(qq + 1) * TB], in_=l_m2),
            incs=[(sem["dLN1"], 16)])
        dlc = P.count(sem["dLN1"])
        P.emit("sync", lambda e, qq=qq: e.dma_start(
            out=rstd_b[:],
            in_=ln_dram[0:1, qq * TB:(qq + 1) * TB].partition_broadcast(128)),
            waits=[(sem["dLN1"], dlc), (sem["vX1"], 16 * qq)], incs=[(sem["dLN1"], 16)])
        P.emit("sync", lambda e, qq=qq: e.dma_start(
            out=nu_b[:],
            in_=ln_dram[1:2, qq * TB:(qq + 1) * TB].partition_broadcast(128)),
            incs=[(sem["dLN1"], 16)])
        dl1 = P.count(sem["dLN1"])
        for c2 in range(NCH):
            k = qq * NCH + c2
            rt = r1t_sb[k % 2]
            rw = [(sem["dLN1"], dl1), (sem["dR1w"], 256 * (qq + 1)),
                  (sem["pR1"], 16 * (qq + 1))]
            if k >= 2:
                rw.append((sem["vX1"], k - 1))
            P.emit("sync", lambda e, rt=rt, c2=c2, qq=qq: e.dma_start(
                out=rt[:], in_=r1_dram[c2 * 128:(c2 + 1) * 128, qq * TB:(qq + 1) * TB]),
                waits=rw, incs=[(sem["dR1r"], 16)])
            drc = P.count(sem["dR1r"])
            xm = x1m_sb[k % 2]
            xv = x1t_sb[k % 2]
            P.emit("vector", lambda e, xm=xm, rt=rt: e.tensor_mul(
                xm[:], rt[:], rstd_b[:]),
                waits=[(sem["dR1r"], drc)])
            xw2 = []
            if k >= 2:
                xw2 = [(sem["dX1w"], 16 * (k - 1)), (sem["pX1"], k - 1)]
            P.emit("vector", lambda e, xv=xv, xm=xm: e.tensor_sub(
                xv[:], xm[:], nu_b[:]),
                waits=xw2, incs=[(sem["vX1"], 1)])
            vxc = P.count(sem["vX1"])
            P.emit("sync", lambda e, xv=xv, c2=c2, qq=qq: e.dma_start(
                out=x1_dram[c2 * 128:(c2 + 1) * 128, qq * TB:(qq + 1) * TB], in_=xv[:]),
                waits=[(sem["vX1"], vxc)], incs=[(sem["dX1w"], 16)])
            P.emit("gpsimd", lambda e, xv=xv, c2=c2, qq=qq: e.tensor_copy(
                x1q(c2, qq), xv[:]),
                waits=[(sem["vX1"], vxc)], incs=[(sem["pX1"], 1)])

    n_pair = 0
    n_tile = 0
    n_evac = 0

    for t in range(NT):
        nkt = 4 * (t + 1)
        q0 = QT * t
        for p in range(2):
            base = n_tile

            def emit_qk(j, tile_idx, first2=False, q0=q0, p=p):
                sps = s_ps[tile_idx % 2]
                waits = []
                if tile_idx >= 2:
                    waits.append((sem["xp"], tile_idx - 1))
                if first2:
                    waits.append((sem["rc"], P.count(sem["rc"])))
                for hh in (0, 1):
                    h = 2 * p + hh
                    P.emit("tensor", lambda e, sps=sps, j=j, hh=hh, h=h, q0=q0: e.matmul(
                        sps[:, hh * QT:(hh + 1) * QT],
                        k_all[:, j * 128:(j + 1) * 128],
                        q_all[:, h * T + q0: h * T + q0 + QT],
                        start=True, stop=True, skip_group_check=True),
                        waits=waits if hh == 0 else [],
                        incs=[(sem["sp"], 1)] if hh == 1 else [])

            def emit_exp(tile_idx):
                sps = s_ps[tile_idx % 2]
                pb = p_sb[tile_idx % 3]
                waits = [(sem["sp"], tile_idx + 1)]
                if tile_idx >= 3:
                    waits.append((sem["pv"], tile_idx - 2))
                    waits.append((sem["za"], tile_idx - 2))
                P.emit("scalar", lambda e, sps=sps, pb=pb: e.activation(
                    pb[:], sps[:], AF.Exp, bias=c_exp, scale=EXP_SCALE),
                    waits=waits, incs=[(sem["xp"], 1)])

            def emit_mask(j, tile_idx, t=t):
                o = j - 4 * t
                pb = p_sb[tile_idx % 3]
                m = masks_sb[:, o * QT:(o + 1) * QT]
                mrep = dataclasses.replace(
                    m, ap=mybir.VecI64Pair([list(m.ap[0]), [0, 2], list(m.ap[1])]))
                pb3 = pb[:].rearrange("a (r n) -> a r n", r=2)
                P.emit("vector", lambda e, pb3=pb3, mrep=mrep: e.tensor_mul(pb3, pb3, mrep),
                       waits=[(sem["xp"], tile_idx + 1)], incs=[(sem["mk"], 1)])

            def emit_pvz(j, tile_idx, is_first, is_last, masked, n_pair=n_pair):
                pb = p_sb[tile_idx % 3]
                waits = ([(sem["mk"], P.count(sem["mk"]))] if masked
                         else [(sem["xp"], tile_idx + 1)])
                if is_first and n_pair >= 2:
                    waits = waits + [(sem["ev"], 2 * n_pair - 2)]
                for hh in (0, 1):
                    ob = o_ps[(n_pair % 2) * 2 + hh]
                    P.emit("tensor", lambda e, pb=pb, j=j, hh=hh, ob=ob, a=is_first, z=is_last: e.matmul(
                        ob[:], v_all[:, j * 128:(j + 1) * 128], pb[:, hh * QT:(hh + 1) * QT],
                        start=a, stop=z, skip_group_check=True),
                        waits=waits if hh == 0 else [],
                        incs=[(sem["pv"], 1)] if hh == 1 else [])

            def emit_zadd(j, tile_idx, is_first, masked, n_pair=n_pair):
                pb = p_sb[tile_idx % 3]
                zb = zacc_sb[n_pair % 2]
                waits = ([(sem["mk"], P.count(sem["mk"]))] if masked
                         else [(sem["xp"], tile_idx + 1)])
                if is_first and n_pair >= 2:
                    waits = waits + [(sem["zm"], 2 * (n_pair - 1))]
                if is_first:
                    P.emit("vector", lambda e, zb=zb, pb=pb: e.tensor_copy(zb[:], pb[:]),
                           waits=waits, incs=[(sem["za"], 1)])
                else:
                    P.emit("vector", lambda e, zb=zb, pb=pb: e.tensor_add(zb[:], zb[:], pb[:]),
                           waits=waits, incs=[(sem["za"], 1)])

            emit_qk(0, base, first2=True)
            if nkt > 1:
                emit_qk(1, base + 1, first2=True)
            for j in range(nkt):
                ti = base + j
                masked = j >= nkt - 4
                emit_exp(ti)
                if masked:
                    emit_mask(j, ti)
                emit_pvz(j, ti, j == 0, j == nkt - 1, masked)
                emit_zadd(j, ti, j == 0, masked)
                if j + 2 < nkt:
                    emit_qk(j + 2, ti + 2)
            n_tile += nkt

            zac = P.count(sem["za"])
            zmb = zacc_sb[n_pair % 2]
            for hh in (0, 1):
                zw0 = [(sem["za"], zac), (sem["rc"], 2 * n_pair)] if hh == 0 else []
                P.emit("tensor", lambda e, hh=hh, zmb=zmb: e.matmul(
                    z_ps[hh], c_ones32, zmb[:, hh * QT:(hh + 1) * QT],
                    start=True, stop=True, skip_group_check=True),
                    waits=zw0, incs=[(sem["zm"], 1)])
            zmc = P.count(sem["zm"])
            for hh in (0, 1):
                h = 2 * p + hh
                zw = [(sem["zm"], zmc)]
                if n_evac >= 2:
                    zw.append((sem["dZ"], 32 * (n_evac - 2) + 16))
                P.emit("vector", lambda e, hh=hh, b=n_evac % 2: e.reciprocal(zinv_sb[b][:], z_ps[hh]),
                       waits=zw, incs=[(sem["rc"], 1)])
                rcc = P.count(sem["rc"])
                row = t * HPC + h
                P.emit("sync", lambda e, b=n_evac % 2, row=row: e.dma_start(
                    out=zscr[row:row + 1, :], in_=zinv_sb[b][:]),
                    waits=[(sem["rc"], rcc)], incs=[(sem["dZ"], 16)])
                dz1 = P.count(sem["dZ"])
                bw = [(sem["dZ"], dz1)]
                if n_evac >= 2:
                    bw.append((sem["ev"], n_evac - 1))
                P.emit("sync", lambda e, b=n_evac % 2, row=row: e.dma_start(
                    out=zb_sb[b][:], in_=zscr[row:row + 1, :].partition_broadcast(128)),
                    waits=bw, incs=[(sem["dZ"], 16)])
                dz2 = P.count(sem["dZ"])
                ww = [(sem["dZ"], dz2)]
                if n_evac >= 2:
                    ww.append((sem["dCT"], 16 * (n_evac - 1)))
                ob = o_ps[(n_pair % 2) * 2 + hh]
                P.emit("vector", lambda e, hh=hh, ob=ob, b=n_evac % 2: e.tensor_mul(
                    oo_sb[b][:], ob[:], zb_sb[b][:]),
                    waits=ww, incs=[(sem["ev"], 1)])
                evc = P.count(sem["ev"])
                qq, col = t // 4, (t % 4) * QT
                P.emit("sync", lambda e, h=h, qq=qq, col=col, b=n_evac % 2: e.dma_start(
                    out=contrib[qq][h * 128:(h + 1) * 128, col:col + QT], in_=oo_sb[b][:]),
                    waits=[(sem["ev"], evc)], incs=[(sem["dCT"], 16)])
                n_evac += 1
            n_pair += 1

        if t % 4 == 3:
            qq = t // 4
            ctc = P.count(sem["dCT"])
            P.emit("gpsimd", lambda e, qq=qq: e.collective_compute(
                "AllGather", OP.bypass, [[0, 1, 2, 3], [4, 5, 6, 7]],
                [contrib[qq].opt()], [gathered[qq * C:(qq + 1) * C, :].opt()]),
                waits=[(sem["dCT"], ctc)], incs=[(sem["cc"], 1)])
            P.emit("gpsimd", lambda e: e.tensor_copy(ccd_sb[:], ones_sb[:]),
                   incs=[(sem["ccd"], 1)])

    pp2.close()
    stacks.remove(pp2)
    pp3 = push()
    wo_ps = pp3.enter_context(nc.psum_tensor("wo_ps", [128, TB], F32))
    st_ps = pp3.enter_context(nc.psum_tensor("st_ps", [1, TB], F32))
    for _qq in range(4):
        emit_post(_qq)

    if _DEBUG and _STAGE == 2:
        P.emit("sync", lambda e: e.dma_start(out=dbg["gat"], in_=gathered),
               waits=[(sem["cc"], 4)], incs=[(sem["dbgs"], 16)])
    if _STAGE == 2:
        P.wait("sync", sem["dbgs"], P.count(sem["dbgs"]))
        P.wait("gpsimd", sem["cc"], 4)
        return finalize(nc, P, stacks)

    x1_ready = P.count(sem["pX1"])
    nx1w = P.count(sem["dX1w"])

    pp3.close()
    stacks.remove(pp3)
    ppffa = push()
    g_ps = [ppffa.enter_context(nc.psum_tensor(f"g_ps{m}", [128, TB], F32)) for m in range(4)]
    u_ps = [ppffa.enter_context(nc.psum_tensor(f"u_ps{m}", [128, TB], F32)) for m in range(4)]

    sA2.close()
    stacks.remove(sA2)
    sW.close()
    stacks.remove(sW)
    sC.close()
    stacks.remove(sC)
    sG = push()
    NSL = 11
    wgu_sb = [sG.enter_context(nc.sbuf_tensor(f"wgu{i}", [128, NCH * 128], BF16)) for i in range(NSL)]
    sg_sb = sG.enter_context(nc.sbuf_tensor("sg_sb", [128, 4 * TB], F32))
    hf_sb = sG.enter_context(nc.sbuf_tensor("hf_sb", [128, 4 * TB], F32))
    mma_sb = sG.enter_context(nc.sbuf_tensor("mma_sb", [128, 3 * TB], F32))
    t2_sb = sG.enter_context(nc.sbuf_tensor("t2_sb", [128, TB], F32))
    msk_sb = sG.enter_context(nc.sbuf_tensor("msk_sb", [128, 4 * TB], F32))
    hsp_sb = [sG.enter_context(nc.sbuf_tensor(f"hspt{i}", [128, TB], BF16)) for i in range(2)]
    wdc_sb = [sG.enter_context(nc.sbuf_tensor(f"wdc{i}", [128, C], BF16)) for i in range(2)]

    def wd_cast(r):
        slot = wdc_sb[r % 2]
        cw = [(sem["pX1"], x1_ready)] if r == 0 else []
        if r >= 2:
            cw.append((sem["dWB"], 32 * (r - 1)))
        P.emit("gpsimd", lambda e, slot=slot, r=r: e.dma_start(
            out=slot[:], in_=wd_h[r * 128:(r + 1) * 128, :]),
            waits=cw, incs=[(sem["dWC"], 16)])
        P.emit("gpsimd", lambda e, slot=slot, r=r: e.dma_start(
            out=wd_bf[r * 128:(r + 1) * 128, :], in_=slot[:, 0:1024]),
            waits=[(sem["dWC"], 16 * (r + 1))], incs=[(sem["dWB"], 16)])
        P.emit("gpsimd", lambda e, slot=slot, r=r: e.dma_start(
            out=wd_bf[I + r * 128:I + (r + 1) * 128, :], in_=slot[:, 1024:2048]),
            incs=[(sem["dWB"], 16)])

    def gu_thresh(cid):
        G, rem = divmod(cid, 8)
        return 8 * (NTB * G + (NTB - 1)) + rem + 1

    def wgu_load(cid):
        G, rem = divmod(cid, 8)
        w, m = divmod(rem, 4)
        src = wg_h if w == 0 else wu_h
        ch = G * 4 + m
        slot = wgu_sb[cid % NSL]
        waits = [(sem["pX1"], x1_ready)] if cid == 0 else []
        if cid >= NSL:
            waits.append((sem["pGU"], gu_thresh(cid - NSL)))
        P.emit("gpsimd", lambda e, slot=slot, src=src, ch=ch: e.dma_start(
            out=slot[:], in_=src[ch * 128:(ch + 1) * 128, :]),
            waits=waits, incs=[(sem["dGU"], 16)])

    n_loaded = 0
    for _ in range(NSL):
        wgu_load(n_loaded)
        n_loaded += 1

    n_spt = 0
    A = mma_sb[:, 0:TB]
    Bm = mma_sb[:, TB:2 * TB]
    Cm = mma_sb[:, 2 * TB:3 * TB]
    for G in range(16):
        for hb in range(NTB):
            it = G * NTB + hb
            tok0 = hb * TB
            for w in range(2):
                for m in range(4):
                    cid = G * 8 + w * 4 + m
                    slot = wgu_sb[cid % NSL]
                    ps = (g_ps if w == 0 else u_ps)[m]
                    waits = [(sem["dGU"], 16 * (cid + 1))]
                    if it >= 1:
                        if w == 0:
                            waits.append((sem["aSI"], 4 * (it - 1) + m + 1))
                        else:
                            waits.append((sem["vH"], 4 * (it - 1) + m + 1))
                    elif w == 0 and m == 0:
                        waits.append((sem["pX1"], x1_ready))
                    for c in range(NCH):
                        P.emit("tensor", lambda e, ps=ps, slot=slot, c=c, hb=hb: e.matmul(
                            ps[:], slot[:, c * 128:(c + 1) * 128],
                            x1q(c, hb),
                            start=(c == 0), stop=(c == NCH - 1), skip_group_check=True),
                            waits=waits if c == 0 else [],
                            incs=[(sem["pGU"], 1)] if c == NCH - 1 else [])
                    if hb == NTB - 1 and n_loaded < 128:
                        wgu_load(n_loaded)
                        n_loaded += 1
            pg = P.count(sem["pGU"])
            for m in range(4):
                sw = [(sem["pGU"], pg - 4)] if m == 0 else []
                if it >= 1:
                    sw.append((sem["vAB"], 4 * (it - 1) + m + 1))
                P.emit("scalar", lambda e, m=m: e.activation(
                    sg_sb[:, m * TB:(m + 1) * TB], g_ps[m][:], AF.Silu),
                    waits=sw, incs=[(sem["aSI"], 1)])
            asic = P.count(sem["aSI"])
            for m in range(4):
                P.emit("vector", lambda e, m=m: e.tensor_mul(
                    hf_sb[:, m * TB:(m + 1) * TB], u_ps[m][:], sg_sb[:, m * TB:(m + 1) * TB]),
                    waits=[(sem["pGU"], pg), (sem["aSI"], asic)] if m == 0 else [],
                    incs=[(sem["vH"], 1)])
            a = [sg_sb[:, m * TB:(m + 1) * TB] for m in range(4)]
            for m in range(4):
                P.emit("vector", lambda e, m=m: e.scalar_tensor_tensor(
                    a[m], hf_sb[:, m * TB:(m + 1) * TB], c_neg1,
                    hf_sb[:, m * TB:(m + 1) * TB], OP.mult, OP.max))
            P.emit("vector", lambda e: e.tensor_max(A, a[0], a[1]))
            P.emit("vector", lambda e: e.tensor_max(Bm, a[2], a[3]))
            P.emit("vector", lambda e: e.tensor_tensor(Cm, A, Bm, OP.min))
            P.emit("vector", lambda e: e.tensor_tensor(A, a[0], a[1], OP.min))
            P.emit("vector", lambda e: e.tensor_tensor(Bm, a[2], a[3], OP.min))
            P.emit("vector", lambda e: e.tensor_max(A, A, Bm))
            P.emit("vector", lambda e: e.tensor_max(t2_sb[:], A, Cm))
            for m in range(4):
                P.emit("vector", lambda e, m=m: e.tensor_tensor(
                    msk_sb[:, m * TB:(m + 1) * TB], a[m], t2_sb[:], OP.is_ge),
                    incs=[(sem["vAB"], 1)])
            for m in range(4):
                buf = hsp_sb[n_spt % 2]
                ww = [(sem["dHW"], 16 * (n_spt - 1))] if n_spt >= 2 else []
                P.emit("vector", lambda e, m=m, buf=buf: e.tensor_mul(
                    buf[:], hf_sb[:, m * TB:(m + 1) * TB], msk_sb[:, m * TB:(m + 1) * TB]),
                    waits=ww, incs=[(sem["vSP"], 1)])
                vspc = P.count(sem["vSP"])
                row0 = hb * I + (G * 4 + m) * 128
                P.emit("sync", lambda e, buf=buf, row0=row0: e.dma_start(
                    out=hsp_dram[row0:row0 + 128, :], in_=buf[:]),
                    waits=[(sem["vSP"], vspc)], incs=[(sem["dHW"], 16)])
                n_spt += 1
        if hb == NTB - 1:
            for rr in range(4 * G, 4 * (G + 1)):
                wd_cast(rr)
    hsp_writes = P.count(sem["dHW"])

    if _DEBUG and _STAGE == 35:
        P.wait("sync", sem["dHW"], hsp_writes)
        P.emit("sync", lambda e: e.dma_start(out=dbg["hsp"], in_=hsp_dram),
               incs=[(sem["dbgs"], 16)])
    if _STAGE == 35:
        P.wait("sync", sem["dbgs"], P.count(sem["dbgs"]))
        P.wait("sync", sem["dHW"], hsp_writes)
        return finalize(nc, P, stacks)

    ppffa.close()
    stacks.remove(ppffa)
    ppffb = push()
    d_ps = [ppffb.enter_context(nc.psum_tensor(f"d_ps{i}", [128, TB], F32)) for i in range(8)]

    sG.close()
    stacks.remove(sG)
    sA.close()
    stacks.remove(sA)
    sH = push()
    NWS = 6
    wd_sb = [sH.enter_context(nc.sbuf_tensor(f"wd{i}", [128, 1024], BF16)) for i in range(NWS)]
    hst_sb = [sH.enter_context(nc.sbuf_tensor(f"hst{i}", [128, TB], BF16)) for i in range(NWS)]
    x1r_sb = [sH.enter_context(nc.sbuf_tensor(f"x1r{i}", [128, TB], F32)) for i in range(8)]
    r2nb_sb = [sH.enter_context(nc.sbuf_tensor(f"r2nb{i}", [128, NCH * TB], F32)) for i in range(2)]
    tmpb_sb = [sH.enter_context(nc.sbuf_tensor(f"tmpb{i}", [128, TB], BF16)) for i in range(2)]

    g0 = [(sem["vSP"], 256), (sem["pGU"], 8 * 64), (sem["dWB"], 32 * 64),
          (sem["dHW"], hsp_writes)]
    n_wd = 0
    n_ev2 = 0
    nfin = 0
    for nb in range(NTB):
        r2b = r2nb_sb[nb % 2]
        for dh in range(2):
            grp = 2 * nb + dh
            for k in range(8):
                d = dh * 8 + k
                x1r = x1r_sb[k]
                xw = (list(g0) if grp == 0 else [(sem["vR2"], 8 * (grp - 1) + k + 1)])
                P.emit("sync", lambda e, x1r=x1r, d=d, nb=nb: e.dma_start(
                    out=x1r[:], in_=x1_dram[d * 128:(d + 1) * 128, nb * TB:(nb + 1) * TB]),
                    waits=xw + [(sem["dX1w"], nx1w)], incs=[(sem["dX1r"], 16)])
            x1_ready = P.count(sem["dX1r"])
            for r in range(64):
                slot = wd_sb[n_wd % NWS]
                hslot = hst_sb[n_wd % NWS]
                ww = list(g0) if n_wd < NWS else [(sem["pWD"], n_wd - (NWS - 1))]
                q = "gpsimd" if (n_wd % 2 == 0) else "sync"
                wdsem = sem["dWD"] if (n_wd % 2 == 0) else sem["dWD2"]
                P.emit(q, lambda e, slot=slot, r=r, dh=dh: e.dma_start(
                    out=slot[:], in_=wd_bf[dh * I + r * 128: dh * I + (r + 1) * 128, :]),
                    waits=ww, incs=[(wdsem, 16)])
                hww = list(g0) if n_wd < NWS else [(sem["pWD"], n_wd - (NWS - 1))]
                P.emit("sync", lambda e, hslot=hslot, r=r, nb=nb: e.dma_start(
                    out=hslot[:], in_=hsp_dram[nb * I + r * 128: nb * I + (r + 1) * 128, :]),
                    waits=hww, incs=[(sem["dHR"], 16)])
                wd_ready = P.count(wdsem)
                hr_ready = P.count(sem["dHR"])
                for dc in range(8):
                    waits = [(wdsem, wd_ready), (sem["dHR"], hr_ready)]
                    if r == 0 and dc == 0 and n_ev2 > 0:
                        waits.append((sem["vR2"], n_ev2))
                    if r == 0 and dc == 0 and dh == 0 and nb >= 1:
                        waits.append((sem["vLN2"], 3 * nb - 2))
                    P.emit("tensor", lambda e, dc=dc, slot=slot, hslot=hslot, r=r: e.matmul(
                        d_ps[dc][:], slot[:, dc * 128:(dc + 1) * 128], hslot[:],
                        start=(r == 0), stop=(r == 63), skip_group_check=True),
                        waits=waits if dc == 0 else [],
                        incs=[(sem["pWD"], 1)] if dc == 7 else [])
                n_wd += 1
            for dc in range(8):
                d = dh * 8 + dc
                x1r = x1r_sb[dc]
                P.emit("vector", lambda e, r2b=r2b, d=d, dc=dc, x1r=x1r: e.tensor_add(
                    r2b[:, d * TB:(d + 1) * TB], d_ps[dc][:], x1r[:]),
                    waits=[(sem["pWD"], n_wd), (sem["dX1r"], x1_ready)],
                    incs=[(sem["vR2"], 1)])
                n_ev2 += 1

        ev_all = P.count(sem["vR2"])
        for c2 in range(NCH):
            sl = r2b[:, c2 * TB:(c2 + 1) * TB]
            tmp = tmpb_sb[c2 % 2]
            sqt = sq_sb[c2 % 2]
            cw = [(sem["vR2"], ev_all)] if c2 == 0 else (
                [(sem["pST2"], 16 * nb + c2 - 1)] if c2 >= 2 else [])
            P.emit("vector", lambda e, tmp=tmp, sl=sl: e.tensor_copy(tmp[:], sl),
                   waits=cw, incs=[(sem["pSQ2"], 1)])
            P.emit("tensor", lambda e, tmp=tmp, c2=c2: e.matmul(
                d_ps[6][0:1, 0:TB], ones_sb[:, 0:1], tmp[:],
                start=(c2 == 0), stop=(c2 == NCH - 1), skip_group_check=True),
                waits=[(sem["pSQ2"], P.count(sem["pSQ2"]))], incs=[(sem["pST2"], 1)])
            sqw = [(sem["zS2"], 16 * nb + c2 - 1)] if c2 >= 2 else []
            P.emit("vector", lambda e, sqt=sqt, tmp=tmp: e.tensor_mul(sqt[:], tmp[:], tmp[:]),
                   waits=sqw, incs=[(sem["aSQ"], 1)])
            P.emit("tensor", lambda e, sqt=sqt, c2=c2: e.matmul(
                d_ps[7][0:1, 0:TB], ones_sb[:, 0:1], sqt[:],
                start=(c2 == 0), stop=(c2 == NCH - 1), skip_group_check=True),
                waits=[(sem["aSQ"], P.count(sem["aSQ"]))], incs=[(sem["zS2"], 1)])

        l_mu = lnt_sb[0:1, 0:TB]
        l_m2 = lnt_sb[0:1, TB:2 * TB]
        l_t = lnt_sb[0:1, 2 * TB:3 * TB]
        l_r = lnt_sb[0:1, 3 * TB:4 * TB]
        stw = [(sem["pST2"], 16 * (nb + 1)), (sem["zS2"], 16 * (nb + 1))]
        if nb >= 1:
            stw.append((sem["dLN2"], 64 * (nb - 1) + 32))
        P.emit("vector", lambda e: e.tensor_scalar(l_mu, d_ps[6][0:1, 0:TB], c_i2048, None, OP.mult),
               waits=stw)
        P.emit("vector", lambda e: e.tensor_scalar(l_m2, d_ps[7][0:1, 0:TB], c_i2048, None, OP.mult))
        P.emit("vector", lambda e: e.tensor_mul(l_t, l_mu, l_mu))
        P.emit("vector", lambda e: e.tensor_sub(l_r, l_m2, l_t), incs=[(sem["vLN2"], 1)])
        vlc = P.count(sem["vLN2"])
        P.emit("scalar", lambda e: e.activation(l_t, l_r, AF.Sqrt, bias=c_eps),
               waits=[(sem["vLN2"], vlc)], incs=[(sem["vLN2"], 1)])
        vlc = P.count(sem["vLN2"])
        P.emit("vector", lambda e: e.reciprocal(l_r, l_t), waits=[(sem["vLN2"], vlc)])
        P.emit("vector", lambda e: e.tensor_mul(l_m2, l_mu, l_r), incs=[(sem["vLN2"], 1)])
        vlc = P.count(sem["vLN2"])
        P.emit("sync", lambda e, nb=nb: e.dma_start(
            out=ln_dram[0:1, nb * TB:(nb + 1) * TB], in_=l_r),
            waits=[(sem["vLN2"], vlc)], incs=[(sem["dLN2"], 16)])
        P.emit("sync", lambda e, nb=nb: e.dma_start(
            out=ln_dram[1:2, nb * TB:(nb + 1) * TB], in_=l_m2),
            incs=[(sem["dLN2"], 16)])
        dlc = P.count(sem["dLN2"])
        P.emit("sync", lambda e, nb=nb: e.dma_start(
            out=rstd_b[:],
            in_=ln_dram[0:1, nb * TB:(nb + 1) * TB].partition_broadcast(128)),
            waits=[(sem["dLN2"], dlc), (sem["vF"], 16 * nb), (sem["vX1"], 64)],
            incs=[(sem["dLN2"], 16)])
        P.emit("sync", lambda e, nb=nb: e.dma_start(
            out=nu_b[:],
            in_=ln_dram[1:2, nb * TB:(nb + 1) * TB].partition_broadcast(128)),
            incs=[(sem["dLN2"], 16)])
        dl_all = P.count(sem["dLN2"])

        for c2 in range(NCH):
            xm = x1m_sb[nfin % 2]
            xv = x1t_sb[nfin % 2]
            fw = [(sem["dLN2"], dl_all)] if c2 == 0 else []
            P.emit("vector", lambda e, xm=xm, r2b=r2b, c2=c2: e.tensor_mul(
                xm[:], r2b[:, c2 * TB:(c2 + 1) * TB], rstd_b[:]),
                waits=fw)
            ow = [(sem["dOUT"], 16 * (nfin - 1))] if nfin >= 2 else []
            P.emit("vector", lambda e, xv=xv, xm=xm: e.tensor_sub(
                xv[:], xm[:], nu_b[:]),
                waits=ow, incs=[(sem["vF"], 1)])
            vfc = P.count(sem["vF"])
            P.emit("sync", lambda e, xv=xv, c2=c2, nb=nb: e.dma_start(
                out=out_fm[c2 * 128:(c2 + 1) * 128, nb * TB:(nb + 1) * TB], in_=xv[:]),
                waits=[(sem["vF"], vfc)], incs=[(sem["dOUT"], 16)])
            nfin += 1

    P.wait("sync", sem["dOUT"], 16 * nfin)
    return finalize(nc, P, stacks)


def finalize(nc, P, stacks):
    with nc.Block() as block:
        @block.tensor
        def _(e):
            P.replay("tensor", e)

        @block.vector
        def _(e):
            P.replay("vector", e)

        @block.scalar
        def _(e):
            P.replay("scalar", e)

        @block.gpsimd
        def _(e):
            P.replay("gpsimd", e)

        @block.sync
        def _(e):
            P.replay("sync", e)
    for s in reversed(stacks):
        try:
            s.close()
        except Exception:
            pass
    return nc


def _host_prep(x, Wq, Wk, Wv, Wo, wgm, wum, wdm):
    x = np.asarray(x, dtype=np.float32)
    Wq = np.asarray(Wq, np.float32); Wk = np.asarray(Wk, np.float32)
    Wv = np.asarray(Wv, np.float32); Wo = np.asarray(Wo, np.float32)
    wgm = np.asarray(wgm, np.float32); wum = np.asarray(wum, np.float32)
    wdm = np.asarray(wdm, np.float32)

    inv = (1.0 / 10000.0 ** (np.arange(0, D, 2, dtype=np.float32) / np.float32(D))).astype(np.float32)
    f = np.arange(T, dtype=np.float32)[:, None] * inv[None, :]
    cos_t = np.ascontiguousarray(np.concatenate([np.cos(f)] * 2, axis=1).T)
    sin_t = np.ascontiguousarray(np.concatenate([np.sin(f)] * 2, axis=1).T)

    masks = np.zeros((128, 4 * QT), dtype=ml_dtypes.bfloat16)
    for o in range(4):
        kk = 128 * o + np.arange(128)[:, None]
        qq = np.arange(QT)[None, :]
        masks[:, o * QT:(o + 1) * QT] = (kk <= qq).astype(ml_dtypes.bfloat16)

    ones_bf = np.ones((128, 1), dtype=ml_dtypes.bfloat16)
    cvec = np.zeros((128, 8), dtype=np.float32)
    cvec[:, 0] = EXP_SHIFT
    cvec[:, 1] = 1e-5
    cvec[:, 2] = 1.0 / 2048.0
    cvec[:, 3] = -1.0
    cvec[:, 4] = 1.0

    Gi, mi, pi = np.meshgrid(np.arange(16), np.arange(4), np.arange(128), indexing="ij")
    idx = (4 * (128 * Gi + pi) + mi).reshape(-1)
    wgp = wgm[:, idx]
    wup = wum[:, idx]
    wd_p = np.ascontiguousarray(wdm[idx, :])
    wg_h = np.ascontiguousarray(
        wgp.reshape(NCH, 128, 64, 128).transpose(2, 1, 0, 3).reshape(64 * 128, NCH * 128))
    wu_h = np.ascontiguousarray(
        wup.reshape(NCH, 128, 64, 128).transpose(2, 1, 0, 3).reshape(64 * 128, NCH * 128))
    wo_h = np.ascontiguousarray(
        Wo.reshape(NCH, 128, NCH, 128).transpose(1, 2, 0, 3).reshape(128, NCH * C))

    xtl = [np.ascontiguousarray(
        x[b].reshape(NT, QT, NCH, 128).transpose(0, 3, 2, 1).reshape(NT * 128, NCH * QT))
        for b in range(B)]

    in_maps = []
    for c in range(8):
        b, g = c // 4, c % 4
        wq_c = Wq[:, 512 * g:512 * (g + 1)]
        wq_h = np.ascontiguousarray(
            wq_c.reshape(NCH, 128, 512).transpose(1, 0, 2).reshape(128, NCH * 512))
        wk_c = Wk[:, 128 * g:128 * (g + 1)]
        wk_hh = np.ascontiguousarray(
            wk_c.reshape(NCH, 128, 128).transpose(1, 0, 2).reshape(128, NCH * 128))
        wv_c = Wv[:, 128 * g:128 * (g + 1)]
        wv_hh = np.ascontiguousarray(
            wv_c.reshape(NCH, 128, 128).transpose(1, 0, 2).reshape(128, NCH * 128))
        toks = np.concatenate([np.arange(qq * TPQ + g * TB, qq * TPQ + (g + 1) * TB)
                               for qq in range(4)])
        in_maps.append({
            "x_tiles": xtl[b],
            "x_res": np.ascontiguousarray(x[b, toks, :].T),
            "wq_h": wq_h, "wk_h": wk_hh, "wv_h": wv_hh,
            "wo_h": wo_h, "wg_h": wg_h, "wu_h": wu_h, "wd_h": wd_p,
            "cos_t": cos_t, "sin_t": sin_t, "masks": masks,
            "ones_bf": ones_bf, "cvec": cvec,
        })
    return in_maps


def kernel(L0_attn_x, L0_attn_Wq, L0_attn_Wk, L0_attn_Wv, L0_attn_Wo,
           L0_ffn_w_gate, L0_ffn_w_up, L0_ffn_w_down):
    in_maps = _host_prep(L0_attn_x, L0_attn_Wq, L0_attn_Wk, L0_attn_Wv, L0_attn_Wo,
                         L0_ffn_w_gate, L0_ffn_w_up, L0_ffn_w_down)
    nc = build_program()
    res = run_bass_kernel_spmd(nc, in_maps, core_ids=list(range(8)))
    out = np.empty((B, T, C), dtype=np.float32)
    for c in range(8):
        b, g = c // 4, c % 4
        o = np.asarray(res.results[c]["out_fm"]).T
        for qq in range(4):
            out[b, qq * TPQ + g * TB: qq * TPQ + (g + 1) * TB, :] = o[qq * TB:(qq + 1) * TB, :]
    return out
